# version 32
# baseline (speedup 1.0000x reference)
import sys

import numpy as np

sys.path.insert(0, "/opt/trn_rl_repo")

import concourse.bass as bass
import concourse.mybir as mybir
from concourse.bass_utils import run_bass_kernel_spmd

import ml_dtypes

F32 = mybir.dt.float32
BF16 = mybir.dt.bfloat16
AF = mybir.ActivationFunctionType
ALU = mybir.AluOpType
AX = mybir.AxisListType

P = 128
D = 1024
S = 4096
BC = 2
NCORES = 8
NCH = 8
NSEG = 11
SEG_N = [384] * 10 + [256]
SEG_OFF = [384 * i for i in range(10)] + [3840]
TAIL_PAD = 32
EPS = 1e-5
SCALE = 0.125

ACT_SET_EXP = 6
ACT_SET_GELU = 10


def build_nc(taps=(), n_seq=BC, n_seg_a=NSEG, do_gf=True, n_seg_b=NSEG, n_cyc=3):
    nc = bass.Bass("TRN2", target_bir_lowering=False)

    dx = nc.dram_tensor("x", [BC, D, S], F32, kind="ExternalInput").ap()
    dy = nc.dram_tensor("y", [BC, D, S], F32, kind="ExternalOutput").ap()
    dypart = nc.dram_tensor("ypart", [BC, D, S], F32).ap()
    dmr = nc.dram_tensor("mr_rows", [BC, NSEG, 2, 384], F32).ap()

    def din(name, shape):
        return nc.dram_tensor(name, shape, F32, kind="ExternalInput").ap()

    def dinb(name, shape):
        return nc.dram_tensor(name, shape, BF16, kind="ExternalInput").ap()

    d_wq = din("wq_t", [D, D]); d_wk = din("wk_t", [D, D])
    d_wv = din("wv_t", [D, D]); d_wo = din("wo_t", [D, D])
    d_c1 = din("c1", [D, D]); d_c2 = din("c2", [D, 512])
    d_c3 = din("c3", [512, 256]); d_c4 = din("c4", [256, 1])
    d_g1 = din("g1", [D, 512]); d_g2 = din("g2", [512, 256]); d_g3 = din("g3", [256, 1])
    d_b1c = din("b1c", [P, 8]); d_b2c = din("b2c", [P, 4]); d_b3c = din("b3c", [P, 2])
    d_g1c = din("g1c", [P, 4]); d_g2c = din("g2c", [P, 2])
    d_bvr = din("bvr", [1, D]); d_bor = din("bor", [1, D])
    d_bqr = din("bqr", [1, D]); d_bkr = din("bkr", [1, D])
    d_scal = din("scal", [1, 4])
    d_consts = din("consts", [P, 2])
    d_mask = dinb("mask01", [P, P])
    d_ident = din("identf", [P, P])
    d_onesb = dinb("ones_bf", [1, 384])
    d_onesf = din("ones_f", [1, P])
    d_zerosf = din("zeros_f", [1, P])
    d_onescol = dinb("ones_col", [P, 1])

    import contextlib
    ctx = contextlib.ExitStack()
    sb = lambda name, shape, dt=F32: ctx.enter_context(nc.sbuf_tensor(name, shape, dt))
    psal = lambda name, shape: ctx.enter_context(nc.psum_tensor(name, shape, F32))

    wq = sb("wq", [P, 8, 8, P], BF16); wk = sb("wk", [P, 8, 8, P], BF16)
    wv = sb("wv", [P, 8, 8, P], BF16); wo = sb("wo", [P, 8, 8, P], BF16)
    c1 = sb("c1s", [P, 8, 8, P], BF16); c2 = sb("c2s", [P, 8, 4, P], BF16)
    c3 = sb("c3s", [P, 4, 2, P], BF16); c4 = sb("c4s", [P, 2], BF16)
    b1c = sb("b1c_s", [P, 8]); b2c = sb("b2c_s", [P, 4]); b3c = sb("b3c_s", [P, 2])
    g1c = sb("g1c_s", [P, 4]); g2c = sb("g2c_s", [P, 2])
    bvr = sb("bvr_s", [1, D], BF16); bor = sb("bor_s", [1, D], BF16)
    bqr = sb("bqr_s", [1, D], BF16); bkr = sb("bkr_s", [1, D], BF16)
    sc_in = sb("sc_in", [1, 4])
    consts = sb("consts_s", [P, 2])
    mask01 = sb("mask01_s", [P, P], BF16)
    identf = sb("identf_s", [P, P])
    ones_bf = sb("ones_bf_s", [1, 384], BF16)
    ones_f = sb("ones_f_s", [1, P])
    zeros_f = sb("zeros_f_s", [1, P])
    ones_col = sb("ones_col_s", [P, 1], BF16)

    sx = [sb("sx0", [P, NCH, 384]), sb("sx1", [P, NCH, 384])]
    sbf = [sb("sbf0", [P, NCH, 384], BF16), sb("sbf1", [P, NCH, 384], BF16)]
    s_big = sb("s_big", [P, NCH, 384])
    h1 = sb("h1", [P, NCH, 384], BF16)
    qf = sb("qf", [P, 16, 384], BF16)
    kf = sb("kf", [P, 16, 384], BF16)
    h2 = sb("h2", [P, 4, 384], BF16)
    h3 = sb("h3", [P, 2, 384], BF16)
    vt = sb("vt", [P, 16, 65], BF16)
    at = sb("at", [P, 16, P], BF16)
    atm2 = [sb("atm0", [P, 16, 64]), sb("atm1", [P, 16, 64])]
    rec = sb("rec", [P, 16])
    mrow = [sb("mrow0", [1, 384]), sb("mrow1", [1, 384])]
    rrow = [sb("rrow0", [1, 384]), sb("rrow1", [1, 384])]
    v_row = sb("v_row", [1, 384])
    t_row = sb("t_row", [1, 384])
    sc = sb("sc", [1, 16])
    xnsum = sb("xnsum", [P, NCH])
    xns_t = sb("xns_t", [P, NCH])
    gtmp = sb("gtmp", [P, 16])

    PS = [psal(f"PSb{i}", [P, 512]) for i in range(8)]

    nc.const_aps.register(F32, EPS, consts[:, 0:1])

    SET_EXP = ACT_SET_EXP
    SET_GELU = ACT_SET_GELU

    sstage = s_big[:].rearrange("p a b -> p (a b)")
    stage = [sstage[:, 0:1024], sstage[:, 1024:2048]]

    tw0 = atm2[0][:].rearrange("p a b -> p (a b)")[:, 0:768]
    tw1 = atm2[1][:].rearrange("p a b -> p (a b)")[:, 0:768]

    taps_out = {}
    for t in taps:
        shp = dict(gfsc=[1, 16], gmean=[P, NCH])[t]
        taps_out[t] = nc.dram_tensor("tap_" + t, shp, F32, kind="ExternalOutput").ap()

    with (
        nc.semaphore("s_load") as sem_load,
        nc.semaphore("s_store") as sem_store,
        nc.semaphore("s_pe") as sem_pe,
        nc.semaphore("s_act") as sem_act,
        nc.semaphore("s_dve") as sem_dve,
        nc.Block() as block,
        ctx,
    ):
        cnt = dict(load=0, store=0, pe=0, act=0, dve=0)

        def dve_edge_on(vector, ins):
            cnt["dve"] += 1
            ins.then_inc(sem_dve, 1)
            vector.wait_ge(sem_dve, cnt["dve"])

        def load_table(scalar, set_id):
            scalar.add_instruction(mybir.InstLoadActFuncSet(
                name=nc.get_next_instruction_name(),
                act_func_set_id=set_id, ins=[], outs=[]))

        slabs = []
        for dt_, wt in ((d_wq, wq), (d_wk, wk), (d_wv, wv), (d_wo, wo), (d_c1, c1)):
            for kc in range(8):
                slabs.append((dt_[kc * P:(kc + 1) * P, :],
                              wt[:, kc, :, :].rearrange("p a b -> p (a b)"), 1024))
        for kc in range(8):
            slabs.append((d_c2[kc * P:(kc + 1) * P, :],
                          c2[:, kc, :, :].rearrange("p a b -> p (a b)"), 512))
        for kc in range(4):
            slabs.append((d_c3[kc * P:(kc + 1) * P, :],
                          c3[:, kc, :, :].rearrange("p a b -> p (a b)"), 256))
        for kc in range(2):
            slabs.append((d_c4[kc * P:(kc + 1) * P, :], c4[:, kc:kc + 1], 1))
        slabs.append((d_bvr, "bvr", 1024))
        slabs.append((d_bor, "bor", 1024))
        slabs.append((d_bqr, "bqr", 1024))
        slabs.append((d_bkr, "bkr", 1024))
        NSMALL = 14

        @block.sync
        def _(sync):
            for dst, src in (
                (b1c, d_b1c), (b2c, d_b2c), (b3c, d_b3c), (g1c, d_g1c), (g2c, d_g2c),
                (sc_in, d_scal), (consts, d_consts), (mask01, d_mask),
                (identf, d_ident), (ones_bf, d_onesb), (ones_f, d_onesf),
                (zeros_f, d_zerosf), (ones_col, d_onescol),
            ):
                sync.dma_start(dst[:], src).then_inc(sem_load, 16)
            sync.dma_start(sc[0:1, 8:12], d_scal).then_inc(sem_load, 16)
            cnt["load"] += 16 * NSMALL
            for i, (dsl, _dst, ncols) in enumerate(slabs):
                if i >= 2:
                    j = i - 2
                    if j % 2 == 0:
                        sync.wait_ge(sem_act, j // 2 + 1)
                    else:
                        sync.wait_ge(sem_dve, (j - 1) // 2 + 1)
                sync.wait_ge(sem_load, 16 * (NSMALL + i))
                tgt = stage[i % 2][0:1, 0:ncols] if isinstance(_dst, str) \
                    else stage[i % 2][:, 0:ncols]
                sync.dma_start(tgt, dsl).then_inc(sem_load, 16)
                cnt["load"] += 16

        @block.scalar
        def _(scalar):
            load_table(scalar, SET_EXP)
            scalar.wait_ge(sem_load, 16 * NSMALL)
            for i, (_dsl, dst, ncols) in enumerate(slabs):
                if i % 2 != 0:
                    continue
                scalar.wait_ge(sem_load, 16 * (NSMALL + i + 1))
                if isinstance(dst, str):
                    src = stage[i % 2][0:1, 0:ncols]
                    dst = {"bvr": bvr, "bor": bor, "bqr": bqr, "bkr": bkr}[dst][:]
                else:
                    src = stage[i % 2][:, 0:ncols]
                scalar.copy(dst, src).then_inc(sem_act, 1)
                cnt["act"] += 1

        @block.vector
        def _(vector):
            for i, (_dsl, dst, ncols) in enumerate(slabs):
                if i % 2 != 1:
                    continue
                vector.wait_ge(sem_load, 16 * (NSMALL + i + 1))
                if isinstance(dst, str):
                    src = stage[i % 2][0:1, 0:ncols]
                    dst = {"bvr": bvr, "bor": bor, "bqr": bqr, "bkr": bkr}[dst][:]
                else:
                    src = stage[i % 2][:, 0:ncols]
                vector.tensor_copy(dst, src).then_inc(sem_dve, 1)
                cnt["dve"] += 1

        WEIGHTS_ACT = cnt["act"]
        WEIGHTS_DVE = cnt["dve"]

        @block.vector
        def _(vector):
            vector.wait_ge(sem_load, 16 * NSMALL)
            vector.memset(vt[:, :, 64:65], 1.0)
            vector.memset(xnsum[:], 0.0)
            vector.tensor_scalar_mul(sc[0:1, 0:1], sc_in[0:1, 0:1], 0.5)
            vector.tensor_scalar_mul(sc[0:1, 1:2], sc_in[0:1, 1:2], 0.5)
            vector.tensor_copy(sc[0:1, 2:3], sc_in[0:1, 2:3]).then_inc(sem_dve, 1)
            cnt["dve"] += 1

        rows_store_mark = [0]

        def emit_cast(si, p, x_ld, pe_past):
            n = SEG_N[si]

            @block.vector
            def _(vector, n=n, p=p, xl=x_ld, pp=pe_past):
                vector.wait_ge(sem_load, xl)
                vector.wait_ge(sem_pe, pp)
                ins = vector.tensor_copy(sbf[p][:, :, 0:n], sx[p][:, :, 0:n])
                dve_edge_on(vector, ins)
                vector.tensor_tensor(h1[:, :, 0:n], sbf[p][:, :, 0:n],
                                     sbf[p][:, :, 0:n], ALU.mult).then_inc(sem_dve, 1)
            cnt["dve"] += 1
            return cnt["dve"]

        def emit_stats(b, si, p, cast_ok):
            n = SEG_N[si]
            tf = [t for t in last_tr_evac if t is not None]

            @block.tensor
            def _(tensor, n=n, p=p, cast_ok=cast_ok, tf=tf):
                tensor.wait_ge(sem_dve, cast_ok)
                for t in tf:
                    tensor.wait_ge(*t)
                for kc in range(NCH):
                    tensor.matmul(PS[6][0:1, 0:n], ones_col[:], sbf[p][:, kc, 0:n],
                                  start=(kc == 0), stop=(kc == NCH - 1))
                for kc in range(NCH):
                    ins = tensor.matmul(PS[7][0:1, 0:n], ones_col[:], h1[:, kc, 0:n],
                                        start=(kc == 0), stop=(kc == NCH - 1))
                ins.then_inc(sem_pe, 1)
            cnt["pe"] += 1
            stats_ok = cnt["pe"]

            @block.vector
            def _(vector, n=n, p=p, stats_ok=stats_ok, st_past=rows_store_mark[0]):
                vector.wait_ge(sem_pe, stats_ok)
                vector.wait_ge(sem_store, st_past)
                ins = vector.tensor_scalar_mul(mrow[p][0:1, 0:n], PS[6][0:1, 0:n], 1.0 / D)
                dve_edge_on(vector, ins)
                ins = vector.tensor_tensor(v_row[0:1, 0:n], mrow[p][0:1, 0:n],
                                           mrow[p][0:1, 0:n], ALU.mult)
                dve_edge_on(vector, ins)
                vector.scalar_tensor_tensor(v_row[0:1, 0:n], PS[7][0:1, 0:n], 1.0 / D,
                                            v_row[0:1, 0:n], ALU.mult,
                                            ALU.subtract).then_inc(sem_dve, 1)
            cnt["dve"] += 1
            var_ok = cnt["dve"]
            last_tr_evac[0] = last_tr_evac[1] = (sem_dve, var_ok)

            @block.scalar
            def _(scalar, n=n, p=p, var_ok=var_ok, ae=cnt["act"] + 1):
                scalar.wait_ge(sem_dve, var_ok)
                scalar.activation(rrow[p][0:1, 0:n], v_row[0:1, 0:n], AF.Ln,
                                  bias=EPS).then_inc(sem_act, 1)
                scalar.wait_ge(sem_act, ae)
                scalar.activation(rrow[p][0:1, 0:n], rrow[p][0:1, 0:n], AF.Exp,
                                  scale=-0.5).then_inc(sem_act, 1)
            cnt["act"] += 2
            r_ok = cnt["act"]

            @block.gpsimd
            def _(gpsimd, n=n, p=p, b=b, si=si, r_ok=r_ok, sp=cnt["store"]):
                gpsimd.wait_ge(sem_act, r_ok)
                gpsimd.wait_ge(sem_store, sp)
                gpsimd.dma_start(dmr[b, si, 0, 0:n], mrow[p][0:1, 0:n]).then_inc(sem_store, 16)
                gpsimd.wait_ge(sem_store, sp + 16)
                gpsimd.dma_start(dmr[b, si, 1, 0:n], rrow[p][0:1, 0:n]).then_inc(sem_store, 16)
            cnt["store"] += 32
            rows_store_mark[0] = cnt["store"]
            return r_ok
        def emit_rows_load(b, si, p, store_wait, pe_wait=None):
            n = SEG_N[si]

            @block.sync
            def _(sync, b=b, si=si, n=n, p=p, sw=store_wait, lw=cnt["load"],
                  pw=pe_wait):
                sync.wait_ge(sem_store, sw)
                if pw is not None:
                    sync.wait_ge(sem_pe, pw)
                sync.wait_ge(sem_load, lw)
                sync.dma_start(mrow[p][0:1, 0:n], dmr[b, si, 0, 0:n]).then_inc(sem_load, 16)
                sync.wait_ge(sem_load, lw + 16)
                sync.dma_start(rrow[p][0:1, 0:n], dmr[b, si, 1, 0:n]).then_inc(sem_load, 16)
            cnt["load"] += 32
            return cnt["load"]

        def emit_xload(b, si, p, store_wait, extra_act=None, dve_wait=0):
            n = SEG_N[si]
            off = SEG_OFF[si]

            @block.sync
            def _(sync, b=b, off=off, n=n, p=p, sw=store_wait, ea=extra_act,
                  lw=cnt["load"], dw=dve_wait):
                sync.wait_ge(sem_store, sw)
                if ea is not None:
                    sync.wait_ge(sem_act, ea)
                if dw:
                    sync.wait_ge(sem_dve, dw)
                sync.wait_ge(sem_load, lw)
                sync.dma_start(sx[p][:, :, 0:n],
                               dx[b, :, off:off + n].rearrange("(c p) t -> p c t", p=P)
                               ).then_inc(sem_load, 16)
            cnt["load"] += 16
            return cnt["load"]

        def emit_bcast(si, p, rows_ready, psum_free_dve, extra_act=None):
            n = SEG_N[si]

            @block.tensor
            def _(tensor, n=n, p=p, rl=rows_ready, pf=psum_free_dve, ea=extra_act):
                tensor.wait_ge(rl[0], rl[1])
                tensor.wait_ge(sem_dve, pf)
                if ea is not None:
                    tensor.wait_ge(sem_act, ea)
                tensor.matmul(PS[0][:, 0:n], ones_f[0:1, 0:P], mrow[p][0:1, 0:n],
                              start=True, stop=False, skip_group_check=True)
                tensor.matmul(PS[0][:, 0:n], zeros_f[0:1, 0:P], mrow[p][0:1, 0:n],
                              start=False, stop=True, skip_group_check=True)
                tensor.matmul(PS[1][:, 0:n], ones_f[0:1, 0:P], rrow[p][0:1, 0:n],
                              start=True, stop=False, skip_group_check=True)
                tensor.matmul(PS[1][:, 0:n], zeros_f[0:1, 0:P], rrow[p][0:1, 0:n],
                              start=False, stop=True,
                              skip_group_check=True).then_inc(sem_pe, 1)
            cnt["pe"] += 1
            return cnt["pe"]

        def emit_apply(si, p, fp32_xn, bc_ok, x_ld, sbig_free_act,
                       sbig_free_dve=None, store_wait=0):
            n = SEG_N[si]

            @block.vector
            def _(vector, n=n, p=p, bc=bc_ok, xl=x_ld, sf=sbig_free_act,
                  sd=sbig_free_dve, sw=store_wait):
                vector.wait_ge(sem_pe, bc)
                vector.wait_ge(sem_load, xl)
                if sf:
                    vector.wait_ge(sem_act, sf)
                if sd is not None:
                    vector.wait_ge(sem_dve, sd)
                if sw:
                    vector.wait_ge(sem_store, sw)
                mb = PS[0][:, None, 0:n].broadcast_to([P, NCH, n])
                rb = PS[1][:, None, 0:n].broadcast_to([P, NCH, n])
                t1 = sx[p] if fp32_xn else s_big
                ins = vector.tensor_tensor(t1[:, :, 0:n], sx[p][:, :, 0:n], mb,
                                           ALU.subtract)
                dve_edge_on(vector, ins)
                dst = sx[p] if fp32_xn else sbf[p]
                ins = vector.tensor_tensor(dst[:, :, 0:n], t1[:, :, 0:n], rb, ALU.mult)
                if (not fp32_xn) and si == NSEG - 1:
                    ins = vector.memset(sbf[p][:, :, 256:288], 0.0)
                ins.then_inc(sem_dve, 1)
            cnt["dve"] += 1
            return cnt["dve"]

        def emit_proj(wt, brow, dst, nproj, xn_ready, sbfp, prev_evacs=None,
                      dve_cover=None):
            evac_cnt = []
            dve_at_emit = cnt["dve"] if dve_cover is None else dve_cover
            for mc in range(NCH):
                bank = PS[2 + (mc % 2)]

                @block.tensor
                def _(tensor, mc=mc, bank=bank, nproj=nproj, xn_ready=xn_ready, sbfp=sbfp,
                      dae=dve_at_emit, pv=prev_evacs):
                    if mc == 0:
                        tensor.wait_ge(sem_dve, max(xn_ready, dae))
                    if mc < 2 and pv is not None:
                        tensor.wait_ge(sem_act, pv[mc - 2])
                    if mc >= 2:
                        tensor.wait_ge(sem_act, evac_cnt[mc - 2])
                    for kc in range(NCH):
                        tensor.matmul(bank[:, 0:nproj], wt[:, kc, mc, :],
                                      sbfp[:, kc, 0:nproj],
                                      start=(kc == 0), stop=False, skip_group_check=True)
                    tensor.matmul(bank[:, 0:nproj], brow[0:1, mc * P:(mc + 1) * P],
                                  ones_bf[0:1, 0:nproj], start=False, stop=True,
                                  skip_group_check=True).then_inc(sem_pe, 1)
                cnt["pe"] += 1
                mm_ok = cnt["pe"]

                @block.scalar
                def _(scalar, mc=mc, bank=bank, nproj=nproj, mm_ok=mm_ok):
                    scalar.wait_ge(sem_pe, mm_ok)
                    scalar.copy(dst[0:64, 2 * mc, 0:nproj], bank[0:64, 0:nproj])
                    scalar.copy(dst[0:64, 2 * mc + 1, 0:nproj],
                                bank[64:128, 0:nproj]).then_inc(sem_act, 1)
                cnt["act"] += 1
                evac_cnt.append(cnt["act"])
            return evac_cnt

        def seg_groups(si):
            n = SEG_N[si]
            gs = [("local", g * P, 1, P) for g in range(n // P)]
            nk = P if n == 384 else 96
            gs += [("atrous", d, 3, nk) for d in range(3)]
            return gs

        def gslice(tile, kc, stride, start, nk):
            if stride == 1:
                return tile[:, kc, start:start + nk]
            v = tile[:, kc, 0:nk * 3].rearrange("p (t s) -> p t s", s=3)
            return v[:, :, start]

        last_tr_evac = [None, None]

        def emit_attention(si, sbfp, xn_ready, qk_done, dve_cover=None):
            groups = seg_groups(si)
            prev_norm = 0
            norm_parts = None
            prev_vt_evac = 0
            pend_transp = None

            for gi, (kind, start, stride, nk) in enumerate(groups):
                dve_at_emit = cnt["dve"] if (gi > 0 or dve_cover is None) \
                    else dve_cover

                @block.tensor
                def _(tensor, kind=kind, start=start, stride=stride, nk=nk, gi=gi,
                      np_prev=norm_parts, qk_done=qk_done, dve_at_emit=dve_at_emit):
                    if gi == 0:
                        tensor.wait_ge(sem_act, qk_done)
                        tensor.wait_ge(sem_dve, dve_at_emit)
                    for h in range(16):
                        if h % 4 == 0 and gi > 0:
                            tensor.wait_ge(sem_dve, np_prev[h // 4])
                        lhs = gslice(kf, h, stride, start, nk)[0:64, :]
                        rhs = gslice(qf, h, stride, start, nk)[0:64, :]
                        bank = PS[2 + h // 4]
                        ins = tensor.matmul(bank[0:nk, (h % 4) * P:(h % 4) * P + nk],
                                            lhs, rhs, start=(h % 4 == 0), stop=True,
                                            skip_group_check=True)
                    ins.then_inc(sem_pe, 1)
                cnt["pe"] += 1
                qk_mark = cnt["pe"]

                @block.scalar
                def _(scalar, qk_mark=qk_mark):
                    scalar.wait_ge(sem_pe, qk_mark)
                    for j in range(4):
                        scalar.activation(
                            at[:, 4 * j:4 * j + 4, :],
                            PS[2 + j][:, 0:512].rearrange("p (h q) -> p h q", q=P),
                            AF.Exp, scale=SCALE).then_inc(sem_act, 1)
                cnt["act"] += 4
                exp_mark = cnt["act"]

                @block.vector
                def _(vector, exp_mark=exp_mark):
                    vector.wait_ge(sem_act, exp_mark)
                    mb = mask01[:, None, :].broadcast_to([P, 16, P])
                    vector.tensor_tensor(at[:], at[:], mb, ALU.mult).then_inc(sem_dve, 1)
                cnt["dve"] += 1
                mask_mark = cnt["dve"]

                @block.tensor
                def _(tensor, kind=kind, start=start, stride=stride, nk=nk, gi=gi,
                      lam=xn_ready, pve=prev_vt_evac, sbfp=sbfp):
                    if gi == 0:
                        tensor.wait_ge(sem_dve, lam)
                    else:
                        tensor.wait_ge(sem_act, pve)
                    for half in range(2):
                        bank = PS[half]
                        for kc in range(NCH):
                            lhs = gslice(sbfp, kc, stride, start, nk)
                            tensor.matmul(bank[0:nk, 0:512], lhs,
                                          wv[:, kc, 4 * half:4 * half + 4, :]
                                          .rearrange("p a b -> p (a b)"),
                                          start=(kc == 0), stop=False,
                                          skip_group_check=True)
                        ins = tensor.matmul(bank[0:nk, 0:512], ones_bf[0:1, 0:nk],
                                            bvr[0:1, 512 * half:512 * half + 512],
                                            start=False, stop=True,
                                            skip_group_check=True)
                    ins.then_inc(sem_pe, 1)
                cnt["pe"] += 1
                vt_mark = cnt["pe"]

                @block.scalar
                def _(scalar, nk=nk, vt_mark=vt_mark):
                    scalar.wait_ge(sem_pe, vt_mark)
                    for half in range(2):
                        scalar.activation(
                            vt[0:nk, 8 * half:8 * half + 8, 0:64],
                            PS[half][0:nk, 0:512].rearrange("p (h d) -> p h d", d=64),
                            AF.Copy).then_inc(sem_act, 1)
                cnt["act"] += 2
                prev_vt_evac = cnt["act"]

                if pend_transp is not None:
                    pend_transp()
                    pend_transp = None

                @block.tensor
                def _(tensor, nk=nk, mask_mark=mask_mark, vte=prev_vt_evac):
                    tensor.wait_ge(sem_dve, mask_mark)
                    tensor.wait_ge(sem_act, vte)
                    for h in range(16):
                        bank = PS[2 + h // 4]
                        ins = tensor.matmul(bank[0:nk, (h % 4) * 65:(h % 4) * 65 + 65],
                                            at[0:nk, h, 0:nk], vt[0:nk, h, :],
                                            start=(h % 4 == 0), stop=True,
                                            skip_group_check=True)
                    ins.then_inc(sem_pe, 1)
                cnt["pe"] += 1
                av_mark = cnt["pe"]

                atmg = atm2[gi % 2]

                @block.vector
                def _(vector, nk=nk, av_mark=av_mark, atmg=atmg):
                    vector.wait_ge(sem_pe, av_mark)
                    for j in range(4):
                        v = PS[2 + j][0:nk, 0:260].rearrange("p (h c) -> p h c", c=65)
                        ins = vector.reciprocal(rec[0:nk, 4 * j:4 * j + 4], v[:, :, 64])
                    dve_edge_on(vector, ins)
                    for j in range(4):
                        v = PS[2 + j][0:nk, 0:260].rearrange("p (h c) -> p h c", c=65)
                        rb = rec[0:nk, 4 * j:4 * j + 4][:, :, None].broadcast_to([nk, 4, 64])
                        vector.tensor_tensor(atmg[0:nk, 4 * j:4 * j + 4, :],
                                             v[:, :, 0:64], rb,
                                             ALU.mult).then_inc(sem_dve, 1)
                cnt["dve"] += 4
                norm_parts = [cnt["dve"] - 3 + j for j in range(4)]
                prev_norm = cnt["dve"]

                def make_transp(kind=kind, start=start, nk=nk, norm_parts=norm_parts,
                                atmg=atmg):
                    def go():
                        atm_flat = atmg[:].rearrange("p h d -> p (h d)")
                        evacs = []
                        aal = cnt["act"]
                        for fc in range(NCH):
                            par = fc % 2
                            bank = PS[6 + par]
                            prev_use = last_tr_evac[par]

                            @block.tensor
                            def _(tensor, fc=fc, bank=bank, prev_use=prev_use):
                                tensor.wait_ge(sem_dve, norm_parts[fc // 2])
                                if fc >= 2:
                                    tensor.wait_ge(*evacs[fc - 2])
                                elif prev_use is not None:
                                    tensor.wait_ge(*prev_use)
                                tensor.transpose(bank[:, 0:nk],
                                                 atm_flat[0:nk, fc * P:(fc + 1) * P],
                                                 identf[0:nk, 0:nk]).then_inc(sem_pe, 1)
                            cnt["pe"] += 1
                            tmark = cnt["pe"]
                            if kind == "local":
                                @block.scalar
                                def _(scalar, fc=fc, bank=bank, tmark=tmark):
                                    scalar.wait_ge(sem_pe, tmark)
                                    scalar.activation(s_big[:, fc, start:start + nk],
                                                      bank[:, 0:nk],
                                                      AF.Copy).then_inc(sem_act, 1)
                                cnt["act"] += 1
                                evacs.append((sem_act, cnt["act"]))
                            else:
                                @block.vector
                                def _(vector, fc=fc, bank=bank, tmark=tmark, aal=aal):
                                    vector.wait_ge(sem_pe, tmark)
                                    if fc == 0:
                                        vector.wait_ge(sem_act, aal)
                                    dstv = s_big[:, fc, 0:nk * 3].rearrange(
                                        "p (t s) -> p t s", s=3)[:, :, start]
                                    vector.tensor_tensor(dstv, dstv, bank[:, 0:nk],
                                                         ALU.add).then_inc(sem_dve, 1)
                                cnt["dve"] += 1
                                evacs.append((sem_dve, cnt["dve"]))
                            last_tr_evac[par] = evacs[-1]
                    return go
                pend_transp = make_transp()

            if pend_transp is not None:
                pend_transp()

        sx_store_mark = [0, 0]
        sx_free_dve = [0, 0]
        wo_pe_mark = [0, 0]

        for b in range(n_seq):
            if b > 0:
                @block.scalar
                def _(scalar):
                    load_table(scalar, SET_EXP)

            rows_ld = [0, 0]
            x_ld = [0, 0]
            r_ok2 = [0, 0]
            x_ld[0] = emit_xload(b, 0, 0, sx_store_mark[0],
                                 extra_act=(WEIGHTS_ACT if b == 0 else None),
                                 dve_wait=sx_free_dve[0])
            if b == 0:
                @block.sync
                def _(sync):
                    sync.wait_ge(sem_dve, WEIGHTS_DVE)
            cast0 = emit_cast(0, 0, x_ld[0], cnt["pe"])
            r_ok2[0] = emit_stats(b, 0, 0, cast0)
            bc = emit_bcast(0, 0, (sem_act, r_ok2[0]), cnt["dve"],
                            extra_act=cnt["act"])
            xn_ready = emit_apply(0, 0, False, bc, x_ld[0], cnt["act"],
                                  store_wait=cnt["store"])

            for si in range(n_seg_a):
                n = SEG_N[si]
                off = SEG_OFF[si]
                p = si % 2
                nproj = n + (TAIL_PAD if si == NSEG - 1 else 0)
                nxt = si + 1 if si + 1 < n_seg_a else None

                pe_at_entry = cnt["pe"]
                dve_at_iter = cnt["dve"]

                if nxt is not None:
                    x_ld[1 - p] = emit_xload(b, nxt, 1 - p, sx_store_mark[1 - p],
                                             dve_wait=sx_free_dve[1 - p])

                @block.vector
                def _(vector, n=n, p=p, xr=xn_ready):
                    vector.wait_ge(sem_dve, xr)
                    ins = vector.reduce_sum(xns_t[:, :], sbf[p][:, :, 0:n], axis=AX.X)
                    dve_edge_on(vector, ins)
                    vector.tensor_tensor(xnsum[:], xnsum[:], xns_t[:],
                                         ALU.add).then_inc(sem_dve, 1)
                cnt["dve"] += 1

                if nxt is not None:
                    cast_n = emit_cast(nxt, 1 - p, x_ld[1 - p], pe_at_entry)

                q_evacs = emit_proj(wq, bqr, qf, nproj, xn_ready, sbf[p],
                                    dve_cover=dve_at_iter)
                k_evacs = emit_proj(wk, bkr, kf, nproj, xn_ready, sbf[p],
                                    prev_evacs=q_evacs, dve_cover=dve_at_iter)
                qk_done = k_evacs[-1]

                if nxt is not None:
                    r_ok2[1 - p] = emit_stats(b, nxt, 1 - p, cast_n)

                emit_attention(si, sbf[p], xn_ready, qk_done,
                               dve_cover=dve_at_iter)
                attn_dve, attn_pe = cnt["dve"], cnt["pe"]

                if nxt is not None:
                    bc = emit_bcast(nxt, 1 - p, (sem_act, r_ok2[1 - p]), attn_dve,
                                    extra_act=cnt["act"])

                @block.scalar
                def _(scalar, n=n, p=p, d=attn_dve, pe=attn_pe):
                    scalar.wait_ge(sem_dve, d)
                    scalar.wait_ge(sem_pe, pe)
                    scalar.copy(sbf[p][:, :, 0:n], s_big[:, :, 0:n]).then_inc(sem_act, 1)
                cnt["act"] += 1
                woin_mark = cnt["act"]

                if nxt is not None:
                    xn_ready = emit_apply(nxt, 1 - p, False, bc, x_ld[1 - p], woin_mark)

                wo_evacs = []
                for mc in range(NCH):
                    bank = PS[2 + (mc % 2)]

                    @block.tensor
                    def _(tensor, mc=mc, bank=bank, n=n, p=p, wim=woin_mark):
                        if mc == 0:
                            tensor.wait_ge(sem_act, wim)
                        if mc >= 2:
                            tensor.wait_ge(*wo_evacs[mc - 2])
                        for kc in range(NCH):
                            tensor.matmul(bank[:, 0:n], wo[:, kc, mc, :], sbf[p][:, kc, 0:n],
                                          start=(kc == 0), stop=False, skip_group_check=True)
                        tensor.matmul(bank[:, 0:n], bor[0:1, mc * P:(mc + 1) * P],
                                      ones_bf[0:1, 0:n], start=False, stop=True,
                                      skip_group_check=True).then_inc(sem_pe, 1)
                    cnt["pe"] += 1
                    mm_ok = cnt["pe"]

                    @block.vector
                    def _(vector, mc=mc, bank=bank, n=n, p=p, mm_ok=mm_ok):
                        vector.wait_ge(sem_pe, mm_ok)
                        vector.tensor_tensor(sx[p][:, mc, 0:n], bank[:, 0:n],
                                             sx[p][:, mc, 0:n], ALU.add).then_inc(sem_dve, 1)
                    cnt["dve"] += 1
                    wo_evacs.append((sem_dve, cnt["dve"]))

                @block.gpsimd
                def _(gpsimd, b=b, off=off, n=n, p=p, d=cnt["dve"], sp=cnt["store"]):
                    gpsimd.wait_ge(sem_dve, d)
                    gpsimd.wait_ge(sem_store, sp)
                    gpsimd.dma_start(dypart[b, :, off:off + n].rearrange("(c p) t -> p c t", p=P),
                                     sx[p][:, :, 0:n]).then_inc(sem_store, 16)
                cnt["store"] += 16
                sx_store_mark[p] = cnt["store"]

            if not do_gf:
                break
            @block.scalar
            def _(scalar):
                load_table(scalar, SET_GELU)

            @block.vector
            def _(vector):
                vector.tensor_scalar_mul(xnsum[:], xnsum[:], 1.0 / S).then_inc(sem_dve, 1)
            cnt["dve"] += 1
            mean_mark = cnt["dve"]

            if b == 0 and "gmean" in taps_out:
                @block.gpsimd
                def _(gpsimd, mm=mean_mark):
                    gpsimd.wait_ge(sem_dve, mm)
                    gpsimd.dma_start(taps_out["gmean"], xnsum[:]).then_inc(sem_store, 16)
                cnt["store"] += 16

            gf_pe_prev = [cnt["pe"], cnt["pe"]]
            gf_stage_par = [0]

            def gf_mm_layer(dram_w, nkc, nmc, rhs_tile, rhs_off, out_bank,
                            rhs_ready_dve=None, rhs_ready_act=None,
                            store_wait=0):
                for kc in range(nkc):
                    sg = gf_stage_par[0]
                    gf_stage_par[0] ^= 1

                    @block.sync
                    def _(sync, kc=kc, sg=sg, pe_need=gf_pe_prev[sg], nmc=nmc,
                          lw=cnt["load"], sw=store_wait):
                        sync.wait_ge(sem_pe, pe_need)
                        if kc == 0 and sw:
                            sync.wait_ge(sem_store, sw)
                        sync.wait_ge(sem_load, lw)
                        sync.dma_start(stage[sg][:, 0:nmc * P],
                                       dram_w[kc * P:(kc + 1) * P, :]).then_inc(sem_load, 16)
                    cnt["load"] += 16
                    lm = cnt["load"]

                    @block.tensor
                    def _(tensor, kc=kc, sg=sg, lm=lm, nmc=nmc, nkc=nkc,
                          rhs_off=rhs_off, rrd=rhs_ready_dve, rra=rhs_ready_act):
                        tensor.wait_ge(sem_load, lm)
                        if rrd is not None and kc == 0:
                            tensor.wait_ge(sem_dve, rrd)
                        if rra is not None and kc == 0:
                            tensor.wait_ge(sem_act, rra)
                        for mc in range(nmc):
                            ins = tensor.matmul(out_bank[:, mc:mc + 1],
                                                stage[sg][:, mc * P:(mc + 1) * P],
                                                rhs_tile[:, rhs_off + kc:rhs_off + kc + 1],
                                                start=(kc == 0 and mc == 0),
                                                stop=(kc == nkc - 1),
                                                skip_group_check=True)
                        ins.then_inc(sem_pe, 1)
                    cnt["pe"] += 1
                    gf_pe_prev[sg] = cnt["pe"]
                return cnt["pe"]

            l1 = gf_mm_layer(d_g1, 8, 4, xnsum, 0, PS[0], rhs_ready_dve=mean_mark,
                             store_wait=(cnt["store"] if b > 0 else 0))

            @block.scalar
            def _(scalar, l1=l1):
                scalar.wait_ge(sem_pe, l1)
                for mc in range(4):
                    scalar.activation(gtmp[:, mc:mc + 1], PS[0][:, mc:mc + 1], AF.Gelu,
                                      bias=g1c[:, mc:mc + 1]).then_inc(sem_act, 1)
            cnt["act"] += 4

            l2 = gf_mm_layer(d_g2, 4, 2, gtmp, 0, PS[1], rhs_ready_act=cnt["act"])

            @block.scalar
            def _(scalar, l2=l2):
                scalar.wait_ge(sem_pe, l2)
                for mc in range(2):
                    scalar.activation(gtmp[:, 8 + mc:9 + mc], PS[1][:, mc:mc + 1], AF.Gelu,
                                      bias=g2c[:, mc:mc + 1]).then_inc(sem_act, 1)
            cnt["act"] += 2

            pe_need = cnt["pe"]

            @block.sync
            def _(sync, pe_need=pe_need, lw=cnt["load"]):
                sync.wait_ge(sem_pe, pe_need)
                sync.wait_ge(sem_load, lw)
                sync.dma_start(stage[0][:, 0:1], d_g3[0:P, :]).then_inc(sem_load, 16)
                sync.wait_ge(sem_load, lw + 16)
                sync.dma_start(stage[0][:, 1:2], d_g3[P:2 * P, :]).then_inc(sem_load, 16)
            cnt["load"] += 32
            lm = cnt["load"]

            @block.tensor
            def _(tensor, lm=lm, aa=cnt["act"]):
                tensor.wait_ge(sem_load, lm)
                tensor.wait_ge(sem_act, aa)
                for kc in range(2):
                    ins = tensor.matmul(PS[2][0:1, 0:1], stage[0][:, kc:kc + 1],
                                        gtmp[:, 8 + kc:9 + kc],
                                        start=(kc == 0), stop=(kc == 1),
                                        skip_group_check=True)
                ins.then_inc(sem_pe, 1)
            cnt["pe"] += 1
            u_mark = cnt["pe"]

            @block.scalar
            def _(scalar, um=u_mark):
                scalar.wait_ge(sem_pe, um)
                scalar.activation(sc[0:1, 3:4], PS[2][0:1, 0:1], AF.Tanh,
                                  bias=sc[0:1, 1:2], scale=0.5).then_inc(sem_act, 1)
            cnt["act"] += 1
            gf_mark = cnt["act"]

            @block.vector
            def _(vector, gm=gf_mark):
                vector.wait_ge(sem_act, gm)
                ins = vector.tensor_scalar(sc[0:1, 3:4], sc[0:1, 3:4], 0.5, 0.5,
                                           ALU.mult, ALU.add)
                dve_edge_on(vector, ins)
                ins = vector.tensor_tensor(sc[0:1, 7:8], sc[0:1, 2:3], sc[0:1, 3:4], ALU.mult)
                dve_edge_on(vector, ins)
                ins = vector.scalar_tensor_tensor(sc[0:1, 4:5], sc[0:1, 7:8], 1.4,
                                                  sc[0:1, 2:3], ALU.mult, ALU.subtract)
                dve_edge_on(vector, ins)
                ins = vector.tensor_scalar_add(sc[0:1, 4:5], sc[0:1, 4:5], 0.1)
                dve_edge_on(vector, ins)
                ins = vector.tensor_scalar_mul(sc[0:1, 6:7], sc[0:1, 2:3], 0.3)
                dve_edge_on(vector, ins)
                vector.tensor_tensor(sc[0:1, 5:6], sc[0:1, 4:5], sc[0:1, 6:7], ALU.add)
                vector.memset(xnsum[:], 0.0).then_inc(sem_dve, 1)
            cnt["dve"] += 1
            gf_done = cnt["dve"]

            if b == 0 and "gfsc" in taps_out:
                @block.gpsimd
                def _(gpsimd, gfd=gf_done):
                    gpsimd.wait_ge(sem_dve, gfd)
                    gpsimd.dma_start(taps_out["gfsc"], sc[:]).then_inc(sem_store, 16)
                cnt["store"] += 16

            def emit_cycle(si):
                n = SEG_N[si]
                p = si % 2

                @block.scalar
                def _(scalar, n=n, p=p, orm=out_rd[p]):
                    scalar.wait_ge(sem_dve, orm)
                    scalar.copy(sbf[p][:, :, 0:n], sx[p][:, :, 0:n]).then_inc(sem_act, 1)
                cnt["act"] += 1
                obf_mark = cnt["act"]

                def ffn_layer(wt, src, dst, bias_c, nkc, nmc, in_ready_act):
                    evacs = []
                    for mc in range(nmc):
                        bank = (PS[2], PS[3], PS[6], PS[7])[mc % 4]

                        @block.tensor
                        def _(tensor, mc=mc, bank=bank, n=n, ira=in_ready_act,
                              nkc=nkc, wt=wt, src=src):
                            if mc == 0:
                                tensor.wait_ge(sem_act, ira)
                            if mc >= 4:
                                tensor.wait_ge(sem_act, evacs[mc - 4])
                            for kc in range(nkc):
                                ins = tensor.matmul(bank[:, 0:n], wt[:, kc, mc, :],
                                                    src[:, kc, 0:n],
                                                    start=(kc == 0), stop=(kc == nkc - 1))
                            ins.then_inc(sem_pe, 1)
                        cnt["pe"] += 1
                        mm_ok = cnt["pe"]

                        @block.scalar
                        def _(scalar, mc=mc, bank=bank, n=n, mo=mm_ok, dst=dst,
                              bias_c=bias_c):
                            scalar.wait_ge(sem_pe, mo)
                            scalar.activation(dst[:, mc, 0:n], bank[:, 0:n], AF.Gelu,
                                              bias=bias_c[:, mc:mc + 1]).then_inc(sem_act, 1)
                        cnt["act"] += 1
                        evacs.append(cnt["act"])
                    return cnt["act"]

                m1 = ffn_layer(c1, sbf[p], h1, b1c, 8, 8, obf_mark)
                m2 = ffn_layer(c2, h1, h2, b2c, 8, 4, m1)
                m3 = ffn_layer(c3, h2, h3, b3c, 4, 2, m2)

                @block.tensor
                def _(tensor, n=n, m3=m3):
                    tensor.wait_ge(sem_act, m3)
                    for kc in range(2):
                        ins = tensor.matmul(PS[4][0:1, 0:n], c4[:, kc:kc + 1],
                                            h3[:, kc, 0:n],
                                            start=(kc == 0), stop=(kc == 1))
                    ins.then_inc(sem_pe, 1)
                cnt["pe"] += 1
                z_mark = cnt["pe"]

                @block.scalar
                def _(scalar, n=n, zm=z_mark):
                    scalar.wait_ge(sem_pe, zm)
                    scalar.activation(t_row[0:1, 0:n], PS[4][0:1, 0:n], AF.Tanh,
                                      bias=sc[0:1, 0:1], scale=0.5).then_inc(sem_act, 1)
                cnt["act"] += 1
                tr_mark = cnt["act"]

                @block.vector
                def _(vector, n=n, tm=tr_mark):
                    vector.wait_ge(sem_act, tm)
                    vector.tensor_scalar(t_row[0:1, 0:n], t_row[0:1, 0:n],
                                         sc[0:1, 6:7], sc[0:1, 5:6],
                                         ALU.mult, ALU.add).then_inc(sem_dve, 1)
                cnt["dve"] += 1
                cp_mark = cnt["dve"]

                @block.tensor
                def _(tensor, n=n, cm=cp_mark):
                    tensor.wait_ge(sem_dve, cm)
                    tensor.matmul(PS[5][:, 0:n], ones_f[0:1, 0:P], t_row[0:1, 0:n],
                                  start=True, stop=False, skip_group_check=True)
                    tensor.matmul(PS[5][:, 0:n], zeros_f[0:1, 0:P], t_row[0:1, 0:n],
                                  start=False, stop=True,
                                  skip_group_check=True).then_inc(sem_pe, 1)
                cnt["pe"] += 1
                cpb_mark = cnt["pe"]

                @block.vector
                def _(vector, n=n, p=p, cb=cpb_mark):
                    vector.wait_ge(sem_pe, cb)
                    cpb = PS[5][:, None, 0:n].broadcast_to([P, 2, n])
                    q = lambda i: sx[p][:, 2 * i:2 * i + 2, 0:n]
                    t0 = tw0[:, 0:2 * n].rearrange("p (a t) -> p a t", a=2)
                    t1 = tw1[:, 0:2 * n].rearrange("p (a t) -> p a t", a=2)
                    vector.tensor_tensor(t0, cpb, q(0), ALU.mult)
                    ins = vector.tensor_tensor(t1, cpb, q(2), ALU.mult)
                    dve_edge_on(vector, ins)
                    vector.tensor_tensor(q(0), q(0), t1, ALU.add)
                    ins = vector.tensor_tensor(q(2), q(2), t0, ALU.subtract)
                    dve_edge_on(vector, ins)
                    vector.tensor_tensor(t0, cpb, q(1), ALU.mult)
                    ins = vector.tensor_tensor(t1, cpb, q(3), ALU.mult)
                    dve_edge_on(vector, ins)
                    vector.tensor_tensor(q(1), q(1), t1, ALU.subtract)
                    ins = vector.tensor_tensor(q(3), q(3), t0, ALU.add)
                    ins.then_inc(sem_dve, 1)
                cnt["dve"] += 1
                out_rd[p] = cnt["dve"]

            def emit_ypart(si, store_wait, pe_wait=None):
                n = SEG_N[si]
                off = SEG_OFF[si]

                @block.sync
                def _(sync, b=b, off=off, n=n, sw=store_wait, pw=pe_wait,
                      lw=cnt["load"]):
                    sync.wait_ge(sem_store, sw)
                    if pw is not None:
                        sync.wait_ge(sem_pe, pw)
                    sync.wait_ge(sem_load, lw)
                    sync.dma_start(s_big[:, :, 0:n],
                                   dypart[b, :, off:off + n].rearrange("(c p) t -> p c t", p=P)
                                   ).then_inc(sem_load, 16)
                cnt["load"] += 16
                return cnt["load"]

            def emit_final(si, ypl):
                n = SEG_N[si]
                p = si % 2

                @block.vector
                def _(vector, n=n, p=p, orm=out_rd[p], ypl=ypl):
                    vector.wait_ge(sem_dve, orm)
                    vector.wait_ge(sem_load, ypl)
                    vector.tensor_tensor(s_big[:, :, 0:n], sx[p][:, :, 0:n],
                                         s_big[:, :, 0:n], ALU.add).then_inc(sem_dve, 1)
                cnt["dve"] += 1
                return cnt["dve"]

            def emit_ystore(si, final_dve):
                n = SEG_N[si]
                off = SEG_OFF[si]

                @block.gpsimd
                def _(gpsimd, b=b, off=off, n=n, d=final_dve, sp=cnt["store"]):
                    gpsimd.wait_ge(sem_dve, d)
                    gpsimd.wait_ge(sem_store, sp)
                    gpsimd.dma_start(dy[b, :, off:off + n].rearrange("(c p) t -> p c t", p=P),
                                     s_big[:, :, 0:n]).then_inc(sem_store, 16)
                cnt["store"] += 16
                return cnt["store"]

            rows_ld[0] = emit_rows_load(b, 0, 0, cnt["store"])
            x_ld[0] = emit_xload(b, 0, 0, sx_store_mark[0])
            if n_seg_b > 1:
                rows_ld[1] = emit_rows_load(b, 1, 1, cnt["store"])
                x_ld[1] = emit_xload(b, 1, 1, sx_store_mark[1])
            bc2 = [0, 0]
            out_rd = [0, 0]
            bc2[0] = emit_bcast(0, 0, (sem_load, rows_ld[0]), gf_done, extra_act=cnt["act"])
            out_rd[0] = emit_apply(0, 0, True, bc2[0], x_ld[0], 0)
            if n_seg_b > 1:
                bc2[1] = emit_bcast(1, 1, (sem_load, rows_ld[1]), out_rd[0], extra_act=cnt["act"])
                out_rd[1] = emit_apply(1, 1, True, bc2[1], x_ld[1], 0)
            sbig_store = 0
            sbig_pe = u_mark
            final_dve = [0, 0]

            for s0 in range(0, n_seg_b, 2):
                s1 = s0 + 1 if s0 + 1 < n_seg_b else None
                yp0 = emit_ypart(s0, sbig_store, pe_wait=sbig_pe)
                sbig_pe = None
                for cyc in range(n_cyc):
                    last = (cyc == n_cyc - 1)
                    emit_cycle(s0)
                    if last:
                        final_dve[0] = emit_final(s0, yp0)
                        st0 = emit_ystore(s0, final_dve[0])
                        if s0 + 2 < n_seg_b:
                            rows_ld[0] = emit_rows_load(b, s0 + 2, 0, 0,
                                                        pe_wait=bc2[0])
                            x_ld[0] = emit_xload(b, s0 + 2, 0, 0,
                                                 dve_wait=final_dve[0])
                            bc2[0] = emit_bcast(s0 + 2, 0, (sem_load, rows_ld[0]),
                                                out_rd[1 if s1 is not None else 0],
                                                extra_act=cnt["act"])
                            out_rd[0] = emit_apply(s0 + 2, 0, True, bc2[0],
                                                   x_ld[0], 0)
                    if s1 is not None:
                        emit_cycle(s1)
                        if last:
                            yp1 = emit_ypart(s1, st0)
                            final_dve[1] = emit_final(s1, yp1)
                            sbig_store = emit_ystore(s1, final_dve[1])
                            if s1 + 2 < n_seg_b:
                                rows_ld[1] = emit_rows_load(b, s1 + 2, 1, 0,
                                                            pe_wait=bc2[1])
                                x_ld[1] = emit_xload(b, s1 + 2, 1, 0,
                                                     dve_wait=final_dve[1])
                                bc2[1] = emit_bcast(s1 + 2, 1, (sem_load, rows_ld[1]),
                                                    out_rd[0],
                                                    extra_act=cnt["act"])
                                out_rd[1] = emit_apply(s1 + 2, 1, True, bc2[1],
                                                       x_ld[1], 0)
                if s1 is None:
                    sbig_store = st0
            sx_free_dve[0] = final_dve[0]
            sx_free_dve[1] = final_dve[1] if n_seg_b > 1 else final_dve[0]

        @block.sync
        def _(sync, tot=cnt["store"]):
            sync.wait_ge(sem_store, tot)

    return nc


def _mask01():
    m = np.zeros((P, P), dtype=ml_dtypes.bfloat16)
    for c in range(8):
        m[16 * c:16 * (c + 1), 16 * c:16 * (c + 1)] = 1
    return m


def _prep_inputs(inp):
    f4 = np.float32
    x = np.asarray(inp["L0_attn_x"], dtype=f4)
    Wq = np.asarray(inp["L0_attn_Wq"], dtype=f4); Wk = np.asarray(inp["L0_attn_Wk"], dtype=f4)
    Wv = np.asarray(inp["L0_attn_Wv"], dtype=f4); Wo = np.asarray(inp["L0_attn_Wo"], dtype=f4)
    bq = np.asarray(inp["L0_attn_bq"], dtype=f4); bk = np.asarray(inp["L0_attn_bk"], dtype=f4)
    bv = np.asarray(inp["L0_attn_bv"], dtype=f4); bo = np.asarray(inp["L0_attn_bo"], dtype=f4)

    def col(v, nch):
        return np.ascontiguousarray(np.asarray(v, dtype=f4).reshape(nch, P).T)

    shared = dict(
        wq_t=np.ascontiguousarray(Wq.T), wk_t=np.ascontiguousarray(Wk.T),
        wv_t=np.ascontiguousarray(Wv.T), wo_t=np.ascontiguousarray(Wo.T),
        c1=np.ascontiguousarray(np.asarray(inp["L0_ffn_cn_w1"], dtype=f4)),
        c2=np.ascontiguousarray(np.asarray(inp["L0_ffn_cn_w2"], dtype=f4)),
        c3=np.ascontiguousarray(np.asarray(inp["L0_ffn_cn_w3"], dtype=f4)),
        c4=np.ascontiguousarray(np.asarray(inp["L0_ffn_cn_w4"], dtype=f4).reshape(256, 1)),
        g1=np.ascontiguousarray(np.asarray(inp["L0_ffn_gc_w1"], dtype=f4)),
        g2=np.ascontiguousarray(np.asarray(inp["L0_ffn_gc_w2"], dtype=f4)),
        g3=np.ascontiguousarray(np.asarray(inp["L0_ffn_gc_w3"], dtype=f4).reshape(256, 1)),
        b1c=col(inp["L0_ffn_cn_b1"], 8), b2c=col(inp["L0_ffn_cn_b2"], 4),
        b3c=col(inp["L0_ffn_cn_b3"], 2),
        g1c=col(inp["L0_ffn_gc_b1"], 4), g2c=col(inp["L0_ffn_gc_b2"], 2),
        bvr=np.ascontiguousarray(bv.reshape(1, D)),
        bor=np.ascontiguousarray(bo.reshape(1, D)),
        bqr=np.ascontiguousarray(bq.reshape(1, D)),
        bkr=np.ascontiguousarray(bk.reshape(1, D)),
        scal=np.array([[np.asarray(inp["L0_ffn_cn_b4"], dtype=f4).reshape(-1)[0],
                        np.asarray(inp["L0_ffn_gc_b3"], dtype=f4).reshape(-1)[0],
                        np.asarray(inp["L0_ffn_adaptive_range"], dtype=f4).reshape(-1)[0],
                        0.0]], dtype=f4),
        consts=np.tile(np.array([[EPS, 0.0]], dtype=f4), (P, 1)),
        mask01=_mask01(),
        identf=np.eye(P, dtype=f4),
        ones_bf=np.ones((1, 384), dtype=ml_dtypes.bfloat16),
        ones_f=np.ones((1, P), dtype=f4),
        zeros_f=np.zeros((1, P), dtype=f4),
        ones_col=np.ones((P, 1), dtype=ml_dtypes.bfloat16),
    )
    xfm = np.ascontiguousarray(x.transpose(0, 2, 1))
    return xfm, shared


def kernel(**inputs):
    xfm, shared = _prep_inputs(inputs)
    nc = build_nc()
    in_maps = []
    for i in range(NCORES):
        m = dict(shared)
        m["x"] = np.ascontiguousarray(xfm[2 * i:2 * i + 2])
        in_maps.append(m)
    res = run_bass_kernel_spmd(nc, in_maps, core_ids=list(range(NCORES)))
    y = np.stack([np.asarray(r["y"]) for r in res.results])
    y = y.reshape(16, D, S).transpose(0, 2, 1)
    return np.ascontiguousarray(y)


if __name__ == "__main__":
    import reference as _ref
    inp = _ref.setup_inputs(0)
    inp = {k: np.asarray(v) for k, v in inp.items()}
    out = kernel(**inp)
    want = np.asarray(_ref.reference(**inp))
    d64 = (out.astype(np.float64) - want.astype(np.float64))
    print("var ratio:", float((d64 ** 2).mean() / (want.astype(np.float64) ** 2).mean()))
    print("max abs:", float(np.abs(d64).max()))


# version 33 (final)
# speedup vs baseline: 1.0368x; 1.0368x over previous
import sys

import numpy as np

sys.path.insert(0, "/opt/trn_rl_repo")

import concourse.bass as bass
import concourse.mybir as mybir
from concourse.bass_utils import run_bass_kernel_spmd

import ml_dtypes

F32 = mybir.dt.float32
BF16 = mybir.dt.bfloat16
AF = mybir.ActivationFunctionType
ALU = mybir.AluOpType
AX = mybir.AxisListType

P = 128
D = 1024
S = 4096
BC = 2
NCORES = 8
NCH = 8
NSEG = 11
SEG_N = [384] * 10 + [256]
SEG_OFF = [384 * i for i in range(10)] + [3840]
TAIL_PAD = 32
EPS = 1e-5
SCALE = 0.125

ACT_SET_EXP = 6
ACT_SET_GELU = 10


def build_nc(taps=(), n_seq=BC, n_seg_a=NSEG, do_gf=True, n_seg_b=NSEG, n_cyc=3):
    nc = bass.Bass("TRN2", target_bir_lowering=False)

    dx = nc.dram_tensor("x", [BC, D, S], F32, kind="ExternalInput").ap()
    dy = nc.dram_tensor("y", [BC, D, S], F32, kind="ExternalOutput").ap()
    dypart = nc.dram_tensor("ypart", [BC, D, S], F32).ap()
    dmr = nc.dram_tensor("mr_rows", [BC, NSEG, 2, 384], F32).ap()

    def din(name, shape):
        return nc.dram_tensor(name, shape, F32, kind="ExternalInput").ap()

    def dinb(name, shape):
        return nc.dram_tensor(name, shape, BF16, kind="ExternalInput").ap()

    d_wq = din("wq_t", [D, D]); d_wk = din("wk_t", [D, D])
    d_wv = din("wv_t", [D, D]); d_wo = din("wo_t", [D, D])
    d_c1 = din("c1", [D, D]); d_c2 = din("c2", [D, 512])
    d_c3 = din("c3", [512, 256]); d_c4 = din("c4", [256, 1])
    d_g1 = din("g1", [D, 512]); d_g2 = din("g2", [512, 256]); d_g3 = din("g3", [256, 1])
    d_b1c = din("b1c", [P, 8]); d_b2c = din("b2c", [P, 4]); d_b3c = din("b3c", [P, 2])
    d_g1c = din("g1c", [P, 4]); d_g2c = din("g2c", [P, 2])
    d_bvr = din("bvr", [1, D]); d_bor = din("bor", [1, D])
    d_bqr = din("bqr", [1, D]); d_bkr = din("bkr", [1, D])
    d_scal = din("scal", [1, 4])
    d_consts = din("consts", [P, 2])
    d_mask = dinb("mask01", [P, P])
    d_ident = din("identf", [P, P])
    d_onesb = dinb("ones_bf", [1, 384])
    d_onesf = din("ones_f", [1, P])
    d_zerosf = din("zeros_f", [1, P])
    d_onescol = dinb("ones_col", [P, 1])

    import contextlib
    ctx = contextlib.ExitStack()
    sb = lambda name, shape, dt=F32: ctx.enter_context(nc.sbuf_tensor(name, shape, dt))
    psal = lambda name, shape: ctx.enter_context(nc.psum_tensor(name, shape, F32))

    wq = sb("wq", [P, 8, 8, P], BF16); wk = sb("wk", [P, 8, 8, P], BF16)
    wv = sb("wv", [P, 8, 8, P], BF16); wo = sb("wo", [P, 8, 8, P], BF16)
    c1 = sb("c1s", [P, 8, 8, P], BF16); c2 = sb("c2s", [P, 8, 4, P], BF16)
    c3 = sb("c3s", [P, 4, 2, P], BF16); c4 = sb("c4s", [P, 2], BF16)
    b1c = sb("b1c_s", [P, 8]); b2c = sb("b2c_s", [P, 4]); b3c = sb("b3c_s", [P, 2])
    g1c = sb("g1c_s", [P, 4]); g2c = sb("g2c_s", [P, 2])
    bvr = sb("bvr_s", [1, D], BF16); bor = sb("bor_s", [1, D], BF16)
    bqr = sb("bqr_s", [1, D], BF16); bkr = sb("bkr_s", [1, D], BF16)
    sc_in = sb("sc_in", [1, 4])
    consts = sb("consts_s", [P, 2])
    mask01 = sb("mask01_s", [P, P], BF16)
    identf = sb("identf_s", [P, P])
    ones_bf = sb("ones_bf_s", [1, 384], BF16)
    ones_f = sb("ones_f_s", [1, P])
    zeros_f = sb("zeros_f_s", [1, P])
    ones_col = sb("ones_col_s", [P, 1], BF16)

    sx = [sb("sx0", [P, NCH, 384]), sb("sx1", [P, NCH, 384])]
    sbf = [sb("sbf0", [P, NCH, 384], BF16), sb("sbf1", [P, NCH, 384], BF16)]
    s_big = sb("s_big", [P, NCH, 384])
    h1 = sb("h1", [P, NCH, 384], BF16)
    qf = sb("qf", [P, 16, 384], BF16)
    kf = sb("kf", [P, 16, 384], BF16)
    h2 = sb("h2", [P, 4, 384], BF16)
    h3 = sb("h3", [P, 2, 384], BF16)
    vt = sb("vt", [P, 16, 65], BF16)
    at = sb("at", [P, 16, P], BF16)
    atm2 = [sb("atm0", [P, 16, 64]), sb("atm1", [P, 16, 64])]
    rec = sb("rec", [P, 16])
    mrow = [sb("mrow0", [1, 384]), sb("mrow1", [1, 384])]
    rrow = [sb("rrow0", [1, 384]), sb("rrow1", [1, 384])]
    v_row = sb("v_row", [1, 384])
    t_row = sb("t_row", [1, 384])
    sc = sb("sc", [1, 16])
    xnsum = sb("xnsum", [P, NCH])
    xns_t = sb("xns_t", [P, NCH])
    gtmp = sb("gtmp", [P, 16])

    PS = [psal(f"PSb{i}", [P, 512]) for i in range(8)]

    nc.const_aps.register(F32, EPS, consts[:, 0:1])

    SET_EXP = ACT_SET_EXP
    SET_GELU = ACT_SET_GELU

    sstage = s_big[:].rearrange("p a b -> p (a b)")
    stage = [sstage[:, 0:1024], sstage[:, 1024:2048]]

    tw0 = atm2[0][:].rearrange("p a b -> p (a b)")[:, 0:768]
    tw1 = atm2[1][:].rearrange("p a b -> p (a b)")[:, 0:768]

    taps_out = {}
    for t in taps:
        shp = dict(gfsc=[1, 16], gmean=[P, NCH])[t]
        taps_out[t] = nc.dram_tensor("tap_" + t, shp, F32, kind="ExternalOutput").ap()

    with (
        nc.semaphore("s_load") as sem_load,
        nc.semaphore("s_store") as sem_store,
        nc.semaphore("s_pe") as sem_pe,
        nc.semaphore("s_act") as sem_act,
        nc.semaphore("s_dve") as sem_dve,
        nc.Block() as block,
        ctx,
    ):
        cnt = dict(load=0, store=0, pe=0, act=0, dve=0)

        def dve_edge_on(vector, ins):
            cnt["dve"] += 1
            ins.then_inc(sem_dve, 1)
            vector.wait_ge(sem_dve, cnt["dve"])

        def load_table(scalar, set_id):
            scalar.add_instruction(mybir.InstLoadActFuncSet(
                name=nc.get_next_instruction_name(),
                act_func_set_id=set_id, ins=[], outs=[]))

        slabs = []
        for dt_, wt in ((d_wq, wq), (d_wk, wk), (d_wv, wv), (d_wo, wo), (d_c1, c1)):
            for kc in range(8):
                slabs.append((dt_[kc * P:(kc + 1) * P, :],
                              wt[:, kc, :, :].rearrange("p a b -> p (a b)"), 1024))
        for kc in range(8):
            slabs.append((d_c2[kc * P:(kc + 1) * P, :],
                          c2[:, kc, :, :].rearrange("p a b -> p (a b)"), 512))
        for kc in range(4):
            slabs.append((d_c3[kc * P:(kc + 1) * P, :],
                          c3[:, kc, :, :].rearrange("p a b -> p (a b)"), 256))
        for kc in range(2):
            slabs.append((d_c4[kc * P:(kc + 1) * P, :], c4[:, kc:kc + 1], 1))
        slabs.append((d_bvr, "bvr", 1024))
        slabs.append((d_bor, "bor", 1024))
        slabs.append((d_bqr, "bqr", 1024))
        slabs.append((d_bkr, "bkr", 1024))
        NSMALL = 14

        @block.sync
        def _(sync):
            for dst, src in (
                (b1c, d_b1c), (b2c, d_b2c), (b3c, d_b3c), (g1c, d_g1c), (g2c, d_g2c),
                (sc_in, d_scal), (consts, d_consts), (mask01, d_mask),
                (identf, d_ident), (ones_bf, d_onesb), (ones_f, d_onesf),
                (zeros_f, d_zerosf), (ones_col, d_onescol),
            ):
                sync.dma_start(dst[:], src).then_inc(sem_load, 16)
            sync.dma_start(sc[0:1, 8:12], d_scal).then_inc(sem_load, 16)
            cnt["load"] += 16 * NSMALL
            for i, (dsl, _dst, ncols) in enumerate(slabs):
                if i >= 2:
                    j = i - 2
                    if j % 2 == 0:
                        sync.wait_ge(sem_act, j // 2 + 1)
                    else:
                        sync.wait_ge(sem_dve, (j - 1) // 2 + 1)
                sync.wait_ge(sem_load, 16 * (NSMALL + i))
                tgt = stage[i % 2][0:1, 0:ncols] if isinstance(_dst, str) \
                    else stage[i % 2][:, 0:ncols]
                sync.dma_start(tgt, dsl).then_inc(sem_load, 16)
                cnt["load"] += 16

        @block.scalar
        def _(scalar):
            load_table(scalar, SET_EXP)
            scalar.wait_ge(sem_load, 16 * NSMALL)
            for i, (_dsl, dst, ncols) in enumerate(slabs):
                if i % 2 != 0:
                    continue
                scalar.wait_ge(sem_load, 16 * (NSMALL + i + 1))
                if isinstance(dst, str):
                    src = stage[i % 2][0:1, 0:ncols]
                    dst = {"bvr": bvr, "bor": bor, "bqr": bqr, "bkr": bkr}[dst][:]
                else:
                    src = stage[i % 2][:, 0:ncols]
                scalar.copy(dst, src).then_inc(sem_act, 1)
                cnt["act"] += 1

        @block.vector
        def _(vector):
            for i, (_dsl, dst, ncols) in enumerate(slabs):
                if i % 2 != 1:
                    continue
                vector.wait_ge(sem_load, 16 * (NSMALL + i + 1))
                if isinstance(dst, str):
                    src = stage[i % 2][0:1, 0:ncols]
                    dst = {"bvr": bvr, "bor": bor, "bqr": bqr, "bkr": bkr}[dst][:]
                else:
                    src = stage[i % 2][:, 0:ncols]
                vector.tensor_copy(dst, src).then_inc(sem_dve, 1)
                cnt["dve"] += 1

        WEIGHTS_ACT = cnt["act"]
        WEIGHTS_DVE = cnt["dve"]

        @block.vector
        def _(vector):
            vector.wait_ge(sem_load, 16 * NSMALL)
            vector.memset(vt[:, :, 64:65], 1.0)
            vector.memset(xnsum[:], 0.0)
            vector.tensor_scalar_mul(sc[0:1, 0:1], sc_in[0:1, 0:1], 0.5)
            vector.tensor_scalar_mul(sc[0:1, 1:2], sc_in[0:1, 1:2], 0.5)
            vector.tensor_copy(sc[0:1, 2:3], sc_in[0:1, 2:3]).then_inc(sem_dve, 1)
            cnt["dve"] += 1

        rows_store_mark = [0]

        def emit_cast(si, p, x_ld, pe_past):
            n = SEG_N[si]

            @block.vector
            def _(vector, n=n, p=p, xl=x_ld, pp=pe_past):
                vector.wait_ge(sem_load, xl)
                vector.wait_ge(sem_pe, pp)
                ins = vector.tensor_copy(sbf[p][:, :, 0:n], sx[p][:, :, 0:n])
                dve_edge_on(vector, ins)
                vector.tensor_tensor(h1[:, :, 0:n], sbf[p][:, :, 0:n],
                                     sbf[p][:, :, 0:n], ALU.mult).then_inc(sem_dve, 1)
            cnt["dve"] += 1
            return cnt["dve"]

        def emit_stats(b, si, p, cast_ok):
            n = SEG_N[si]
            tf = [t for t in last_tr_evac if t is not None]

            @block.tensor
            def _(tensor, n=n, p=p, cast_ok=cast_ok, tf=tf):
                tensor.wait_ge(sem_dve, cast_ok)
                for t in tf:
                    tensor.wait_ge(*t)
                for kc in range(NCH):
                    tensor.matmul(PS[6][0:1, 0:n], ones_col[:], sbf[p][:, kc, 0:n],
                                  start=(kc == 0), stop=(kc == NCH - 1))
                for kc in range(NCH):
                    ins = tensor.matmul(PS[7][0:1, 0:n], ones_col[:], h1[:, kc, 0:n],
                                        start=(kc == 0), stop=(kc == NCH - 1))
                ins.then_inc(sem_pe, 1)
            cnt["pe"] += 1
            stats_ok = cnt["pe"]

            @block.vector
            def _(vector, n=n, p=p, stats_ok=stats_ok, st_past=rows_store_mark[0]):
                vector.wait_ge(sem_pe, stats_ok)
                vector.wait_ge(sem_store, st_past)
                ins = vector.tensor_scalar_mul(mrow[p][0:1, 0:n], PS[6][0:1, 0:n], 1.0 / D)
                dve_edge_on(vector, ins)
                ins = vector.tensor_tensor(v_row[0:1, 0:n], mrow[p][0:1, 0:n],
                                           mrow[p][0:1, 0:n], ALU.mult)
                dve_edge_on(vector, ins)
                vector.scalar_tensor_tensor(v_row[0:1, 0:n], PS[7][0:1, 0:n], 1.0 / D,
                                            v_row[0:1, 0:n], ALU.mult,
                                            ALU.subtract).then_inc(sem_dve, 1)
            cnt["dve"] += 1
            var_ok = cnt["dve"]
            last_tr_evac[0] = last_tr_evac[1] = (sem_dve, var_ok)

            @block.scalar
            def _(scalar, n=n, p=p, var_ok=var_ok, ae=cnt["act"] + 1):
                scalar.wait_ge(sem_dve, var_ok)
                scalar.activation(rrow[p][0:1, 0:n], v_row[0:1, 0:n], AF.Ln,
                                  bias=EPS).then_inc(sem_act, 1)
                scalar.wait_ge(sem_act, ae)
                scalar.activation(rrow[p][0:1, 0:n], rrow[p][0:1, 0:n], AF.Exp,
                                  scale=-0.5).then_inc(sem_act, 1)
            cnt["act"] += 2
            r_ok = cnt["act"]

            @block.gpsimd
            def _(gpsimd, n=n, p=p, b=b, si=si, r_ok=r_ok, sp=cnt["store"]):
                gpsimd.wait_ge(sem_act, r_ok)
                gpsimd.wait_ge(sem_store, sp)
                gpsimd.dma_start(dmr[b, si, 0, 0:n], mrow[p][0:1, 0:n]).then_inc(sem_store, 16)
                gpsimd.wait_ge(sem_store, sp + 16)
                gpsimd.dma_start(dmr[b, si, 1, 0:n], rrow[p][0:1, 0:n]).then_inc(sem_store, 16)
            cnt["store"] += 32
            rows_store_mark[0] = cnt["store"]
            return r_ok
        def emit_rows_load(b, si, p, store_wait, pe_wait=None):
            n = SEG_N[si]

            @block.sync
            def _(sync, b=b, si=si, n=n, p=p, sw=store_wait, lw=cnt["load"],
                  pw=pe_wait):
                sync.wait_ge(sem_store, sw)
                if pw is not None:
                    sync.wait_ge(sem_pe, pw)
                sync.wait_ge(sem_load, lw)
                sync.dma_start(mrow[p][0:1, 0:n], dmr[b, si, 0, 0:n]).then_inc(sem_load, 16)
                sync.wait_ge(sem_load, lw + 16)
                sync.dma_start(rrow[p][0:1, 0:n], dmr[b, si, 1, 0:n]).then_inc(sem_load, 16)
            cnt["load"] += 32
            return cnt["load"]

        def emit_xload(b, si, p, store_wait, extra_act=None, dve_wait=0):
            n = SEG_N[si]
            off = SEG_OFF[si]

            @block.sync
            def _(sync, b=b, off=off, n=n, p=p, sw=store_wait, ea=extra_act,
                  lw=cnt["load"], dw=dve_wait):
                sync.wait_ge(sem_store, sw)
                if ea is not None:
                    sync.wait_ge(sem_act, ea)
                if dw:
                    sync.wait_ge(sem_dve, dw)
                sync.wait_ge(sem_load, lw)
                sync.dma_start(sx[p][:, :, 0:n],
                               dx[b, :, off:off + n].rearrange("(c p) t -> p c t", p=P)
                               ).then_inc(sem_load, 16)
            cnt["load"] += 16
            return cnt["load"]

        def emit_bcast(si, p, rows_ready, psum_free_dve, extra_act=None):
            n = SEG_N[si]

            @block.tensor
            def _(tensor, n=n, p=p, rl=rows_ready, pf=psum_free_dve, ea=extra_act):
                tensor.wait_ge(rl[0], rl[1])
                tensor.wait_ge(sem_dve, pf)
                if ea is not None:
                    tensor.wait_ge(sem_act, ea)
                tensor.matmul(PS[0][:, 0:n], ones_f[0:1, 0:P], mrow[p][0:1, 0:n],
                              start=True, stop=False, skip_group_check=True)
                tensor.matmul(PS[0][:, 0:n], zeros_f[0:1, 0:P], mrow[p][0:1, 0:n],
                              start=False, stop=True, skip_group_check=True)
                tensor.matmul(PS[1][:, 0:n], ones_f[0:1, 0:P], rrow[p][0:1, 0:n],
                              start=True, stop=False, skip_group_check=True)
                tensor.matmul(PS[1][:, 0:n], zeros_f[0:1, 0:P], rrow[p][0:1, 0:n],
                              start=False, stop=True,
                              skip_group_check=True).then_inc(sem_pe, 1)
            cnt["pe"] += 1
            return cnt["pe"]

        def emit_apply(si, p, fp32_xn, bc_ok, x_ld, sbig_free_act,
                       sbig_free_dve=None, store_wait=0):
            n = SEG_N[si]

            @block.vector
            def _(vector, n=n, p=p, bc=bc_ok, xl=x_ld, sf=sbig_free_act,
                  sd=sbig_free_dve, sw=store_wait):
                vector.wait_ge(sem_pe, bc)
                vector.wait_ge(sem_load, xl)
                if sf:
                    vector.wait_ge(sem_act, sf)
                if sd is not None:
                    vector.wait_ge(sem_dve, sd)
                if sw:
                    vector.wait_ge(sem_store, sw)
                mb = PS[0][:, None, 0:n].broadcast_to([P, NCH, n])
                rb = PS[1][:, None, 0:n].broadcast_to([P, NCH, n])
                t1 = sx[p] if fp32_xn else s_big
                ins = vector.tensor_tensor(t1[:, :, 0:n], sx[p][:, :, 0:n], mb,
                                           ALU.subtract)
                dve_edge_on(vector, ins)
                dst = sx[p] if fp32_xn else sbf[p]
                ins = vector.tensor_tensor(dst[:, :, 0:n], t1[:, :, 0:n], rb, ALU.mult)
                if (not fp32_xn) and si == NSEG - 1:
                    ins = vector.memset(sbf[p][:, :, 256:288], 0.0)
                ins.then_inc(sem_dve, 1)
            cnt["dve"] += 1
            return cnt["dve"]

        def emit_proj(wt, brow, dst, nproj, xn_ready, sbfp, prev_evacs=None,
                      dve_cover=None):
            evac_cnt = []
            dve_at_emit = cnt["dve"] if dve_cover is None else dve_cover
            for mc in range(NCH):
                bank = PS[2 + (mc % 2)]

                @block.tensor
                def _(tensor, mc=mc, bank=bank, nproj=nproj, xn_ready=xn_ready, sbfp=sbfp,
                      dae=dve_at_emit, pv=prev_evacs):
                    if mc == 0:
                        tensor.wait_ge(sem_dve, max(xn_ready, dae))
                    if mc < 2 and pv is not None:
                        tensor.wait_ge(sem_act, pv[mc - 2])
                    if mc >= 2:
                        tensor.wait_ge(sem_act, evac_cnt[mc - 2])
                    for kc in range(NCH):
                        tensor.matmul(bank[:, 0:nproj], wt[:, kc, mc, :],
                                      sbfp[:, kc, 0:nproj],
                                      start=(kc == 0), stop=False, skip_group_check=True)
                    tensor.matmul(bank[:, 0:nproj], brow[0:1, mc * P:(mc + 1) * P],
                                  ones_bf[0:1, 0:nproj], start=False, stop=True,
                                  skip_group_check=True).then_inc(sem_pe, 1)
                cnt["pe"] += 1
                mm_ok = cnt["pe"]

                @block.scalar
                def _(scalar, mc=mc, bank=bank, nproj=nproj, mm_ok=mm_ok):
                    scalar.wait_ge(sem_pe, mm_ok)
                    scalar.copy(dst[0:64, 2 * mc, 0:nproj], bank[0:64, 0:nproj])
                    scalar.copy(dst[0:64, 2 * mc + 1, 0:nproj],
                                bank[64:128, 0:nproj]).then_inc(sem_act, 1)
                cnt["act"] += 1
                evac_cnt.append(cnt["act"])
            return evac_cnt

        def seg_groups(si):
            n = SEG_N[si]
            gs = [("local", g * P, 1, P) for g in range(n // P)]
            nk = P if n == 384 else 96
            gs += [("atrous", d, 3, nk) for d in range(3)]
            return gs

        def gslice(tile, kc, stride, start, nk):
            if stride == 1:
                return tile[:, kc, start:start + nk]
            v = tile[:, kc, 0:nk * 3].rearrange("p (t s) -> p t s", s=3)
            return v[:, :, start]

        last_tr_evac = [None, None]

        def emit_attention(si, sbfp, xn_ready, qk_done, dve_cover=None):
            groups = seg_groups(si)
            prev_norm = 0
            norm_parts = None
            prev_vt_evac = 0
            pend_transp = None

            for gi, (kind, start, stride, nk) in enumerate(groups):
                dve_at_emit = cnt["dve"] if (gi > 0 or dve_cover is None) \
                    else dve_cover

                @block.tensor
                def _(tensor, kind=kind, start=start, stride=stride, nk=nk, gi=gi,
                      np_prev=norm_parts, qk_done=qk_done, dve_at_emit=dve_at_emit):
                    if gi == 0:
                        tensor.wait_ge(sem_act, qk_done)
                        tensor.wait_ge(sem_dve, dve_at_emit)
                    for h in range(16):
                        if h % 4 == 0 and gi > 0:
                            tensor.wait_ge(sem_dve, np_prev[h // 4])
                        lhs = gslice(kf, h, stride, start, nk)[0:64, :]
                        rhs = gslice(qf, h, stride, start, nk)[0:64, :]
                        bank = PS[2 + h // 4]
                        ins = tensor.matmul(bank[0:nk, (h % 4) * P:(h % 4) * P + nk],
                                            lhs, rhs, start=(h % 4 == 0), stop=True,
                                            skip_group_check=True)
                    ins.then_inc(sem_pe, 1)
                cnt["pe"] += 1
                qk_mark = cnt["pe"]

                @block.scalar
                def _(scalar, qk_mark=qk_mark):
                    scalar.wait_ge(sem_pe, qk_mark)
                    for j in range(4):
                        scalar.activation(
                            at[:, 4 * j:4 * j + 4, :],
                            PS[2 + j][:, 0:512].rearrange("p (h q) -> p h q", q=P),
                            AF.Exp, scale=SCALE).then_inc(sem_act, 1)
                cnt["act"] += 4
                exp_mark = cnt["act"]

                @block.vector
                def _(vector, exp_mark=exp_mark):
                    vector.wait_ge(sem_act, exp_mark)
                    mb = mask01[:, None, :].broadcast_to([P, 16, P])
                    vector.tensor_tensor(at[:], at[:], mb, ALU.mult).then_inc(sem_dve, 1)
                cnt["dve"] += 1
                mask_mark = cnt["dve"]

                @block.tensor
                def _(tensor, kind=kind, start=start, stride=stride, nk=nk, gi=gi,
                      lam=xn_ready, pve=prev_vt_evac, sbfp=sbfp):
                    if gi == 0:
                        tensor.wait_ge(sem_dve, lam)
                    else:
                        tensor.wait_ge(sem_act, pve)
                    for half in range(2):
                        bank = PS[half]
                        for kc in range(NCH):
                            lhs = gslice(sbfp, kc, stride, start, nk)
                            tensor.matmul(bank[0:nk, 0:512], lhs,
                                          wv[:, kc, 4 * half:4 * half + 4, :]
                                          .rearrange("p a b -> p (a b)"),
                                          start=(kc == 0), stop=False,
                                          skip_group_check=True)
                        ins = tensor.matmul(bank[0:nk, 0:512], ones_bf[0:1, 0:nk],
                                            bvr[0:1, 512 * half:512 * half + 512],
                                            start=False, stop=True,
                                            skip_group_check=True)
                    ins.then_inc(sem_pe, 1)
                cnt["pe"] += 1
                vt_mark = cnt["pe"]

                @block.scalar
                def _(scalar, nk=nk, vt_mark=vt_mark):
                    scalar.wait_ge(sem_pe, vt_mark)
                    for half in range(2):
                        scalar.activation(
                            vt[0:nk, 8 * half:8 * half + 8, 0:64],
                            PS[half][0:nk, 0:512].rearrange("p (h d) -> p h d", d=64),
                            AF.Copy).then_inc(sem_act, 1)
                cnt["act"] += 2
                prev_vt_evac = cnt["act"]

                if pend_transp is not None:
                    pend_transp()
                    pend_transp = None

                @block.tensor
                def _(tensor, nk=nk, mask_mark=mask_mark, vte=prev_vt_evac):
                    tensor.wait_ge(sem_dve, mask_mark)
                    tensor.wait_ge(sem_act, vte)
                    for h in range(16):
                        bank = PS[2 + h // 4]
                        ins = tensor.matmul(bank[0:nk, (h % 4) * 65:(h % 4) * 65 + 65],
                                            at[0:nk, h, 0:nk], vt[0:nk, h, :],
                                            start=(h % 4 == 0), stop=True,
                                            skip_group_check=True)
                    ins.then_inc(sem_pe, 1)
                cnt["pe"] += 1
                av_mark = cnt["pe"]

                atmg = atm2[gi % 2]

                @block.vector
                def _(vector, nk=nk, av_mark=av_mark, atmg=atmg):
                    vector.wait_ge(sem_pe, av_mark)
                    for j in range(4):
                        v = PS[2 + j][0:nk, 0:260].rearrange("p (h c) -> p h c", c=65)
                        ins = vector.reciprocal(rec[0:nk, 4 * j:4 * j + 4], v[:, :, 64])
                    dve_edge_on(vector, ins)
                    for j in range(4):
                        v = PS[2 + j][0:nk, 0:260].rearrange("p (h c) -> p h c", c=65)
                        rb = rec[0:nk, 4 * j:4 * j + 4][:, :, None].broadcast_to([nk, 4, 64])
                        vector.tensor_tensor(atmg[0:nk, 4 * j:4 * j + 4, :],
                                             v[:, :, 0:64], rb,
                                             ALU.mult).then_inc(sem_dve, 1)
                cnt["dve"] += 4
                norm_parts = [cnt["dve"] - 3 + j for j in range(4)]
                prev_norm = cnt["dve"]

                def make_transp(kind=kind, start=start, nk=nk, norm_parts=norm_parts,
                                atmg=atmg):
                    def go():
                        atm_flat = atmg[:].rearrange("p h d -> p (h d)")
                        evacs = []
                        aal = cnt["act"]
                        for fc in range(NCH):
                            par = fc % 2
                            bank = PS[6 + par]
                            prev_use = last_tr_evac[par]

                            @block.tensor
                            def _(tensor, fc=fc, bank=bank, prev_use=prev_use):
                                tensor.wait_ge(sem_dve, norm_parts[fc // 2])
                                if fc >= 2:
                                    tensor.wait_ge(*evacs[fc - 2])
                                elif prev_use is not None:
                                    tensor.wait_ge(*prev_use)
                                tensor.transpose(bank[:, 0:nk],
                                                 atm_flat[0:nk, fc * P:(fc + 1) * P],
                                                 identf[0:nk, 0:nk]).then_inc(sem_pe, 1)
                            cnt["pe"] += 1
                            tmark = cnt["pe"]
                            if kind == "local":
                                @block.scalar
                                def _(scalar, fc=fc, bank=bank, tmark=tmark):
                                    scalar.wait_ge(sem_pe, tmark)
                                    scalar.activation(s_big[:, fc, start:start + nk],
                                                      bank[:, 0:nk],
                                                      AF.Copy).then_inc(sem_act, 1)
                                cnt["act"] += 1
                                evacs.append((sem_act, cnt["act"]))
                            else:
                                @block.vector
                                def _(vector, fc=fc, bank=bank, tmark=tmark, aal=aal):
                                    vector.wait_ge(sem_pe, tmark)
                                    if fc == 0:
                                        vector.wait_ge(sem_act, aal)
                                    dstv = s_big[:, fc, 0:nk * 3].rearrange(
                                        "p (t s) -> p t s", s=3)[:, :, start]
                                    vector.tensor_tensor(dstv, dstv, bank[:, 0:nk],
                                                         ALU.add).then_inc(sem_dve, 1)
                                cnt["dve"] += 1
                                evacs.append((sem_dve, cnt["dve"]))
                            last_tr_evac[par] = evacs[-1]
                    return go
                pend_transp = make_transp()

            if pend_transp is not None:
                pend_transp()

        sx_store_mark = [0, 0]
        sx_free_dve = [0, 0]
        wo_pe_mark = [0, 0]

        for b in range(n_seq):
            if b > 0:
                @block.scalar
                def _(scalar):
                    load_table(scalar, SET_EXP)

            rows_ld = [0, 0]
            x_ld = [0, 0]
            r_ok2 = [0, 0]
            x_ld[0] = emit_xload(b, 0, 0, sx_store_mark[0],
                                 extra_act=(WEIGHTS_ACT if b == 0 else None),
                                 dve_wait=sx_free_dve[0])
            if b == 0:
                @block.sync
                def _(sync):
                    sync.wait_ge(sem_dve, WEIGHTS_DVE)
            cast0 = emit_cast(0, 0, x_ld[0], cnt["pe"])
            r_ok2[0] = emit_stats(b, 0, 0, cast0)
            bc = emit_bcast(0, 0, (sem_act, r_ok2[0]), cnt["dve"],
                            extra_act=cnt["act"])
            xn_ready = emit_apply(0, 0, False, bc, x_ld[0], cnt["act"],
                                  store_wait=cnt["store"])

            for si in range(n_seg_a):
                n = SEG_N[si]
                off = SEG_OFF[si]
                p = si % 2
                nproj = n + (TAIL_PAD if si == NSEG - 1 else 0)
                nxt = si + 1 if si + 1 < n_seg_a else None

                pe_at_entry = cnt["pe"]
                dve_at_iter = cnt["dve"]

                if nxt is not None:
                    x_ld[1 - p] = emit_xload(b, nxt, 1 - p, sx_store_mark[1 - p],
                                             dve_wait=sx_free_dve[1 - p])

                @block.vector
                def _(vector, n=n, p=p, xr=xn_ready):
                    vector.wait_ge(sem_dve, xr)
                    ins = vector.reduce_sum(xns_t[:, :], sbf[p][:, :, 0:n], axis=AX.X)
                    dve_edge_on(vector, ins)
                    vector.tensor_tensor(xnsum[:], xnsum[:], xns_t[:],
                                         ALU.add).then_inc(sem_dve, 1)
                cnt["dve"] += 1

                if nxt is not None:
                    cast_n = emit_cast(nxt, 1 - p, x_ld[1 - p], pe_at_entry)

                q_evacs = emit_proj(wq, bqr, qf, nproj, xn_ready, sbf[p],
                                    dve_cover=dve_at_iter)
                k_evacs = emit_proj(wk, bkr, kf, nproj, xn_ready, sbf[p],
                                    prev_evacs=q_evacs, dve_cover=dve_at_iter)
                qk_done = k_evacs[-1]

                if nxt is not None:
                    r_ok2[1 - p] = emit_stats(b, nxt, 1 - p, cast_n)

                emit_attention(si, sbf[p], xn_ready, qk_done,
                               dve_cover=dve_at_iter)
                attn_dve, attn_pe = cnt["dve"], cnt["pe"]

                if nxt is not None:
                    bc = emit_bcast(nxt, 1 - p, (sem_act, r_ok2[1 - p]), xn_ready,
                                    extra_act=cnt["act"])

                @block.scalar
                def _(scalar, n=n, p=p, d=attn_dve, pe=attn_pe):
                    scalar.wait_ge(sem_dve, d)
                    scalar.wait_ge(sem_pe, pe)
                    scalar.copy(sbf[p][:, :, 0:n], s_big[:, :, 0:n]).then_inc(sem_act, 1)
                cnt["act"] += 1
                woin_mark = cnt["act"]

                if nxt is not None:
                    xn_ready = emit_apply(nxt, 1 - p, False, bc, x_ld[1 - p], woin_mark)

                wo_evacs = []
                for mc in range(NCH):
                    bank = PS[2 + (mc % 2)]

                    @block.tensor
                    def _(tensor, mc=mc, bank=bank, n=n, p=p, wim=woin_mark):
                        if mc == 0:
                            tensor.wait_ge(sem_act, wim)
                        if mc >= 2:
                            tensor.wait_ge(*wo_evacs[mc - 2])
                        for kc in range(NCH):
                            tensor.matmul(bank[:, 0:n], wo[:, kc, mc, :], sbf[p][:, kc, 0:n],
                                          start=(kc == 0), stop=False, skip_group_check=True)
                        tensor.matmul(bank[:, 0:n], bor[0:1, mc * P:(mc + 1) * P],
                                      ones_bf[0:1, 0:n], start=False, stop=True,
                                      skip_group_check=True).then_inc(sem_pe, 1)
                    cnt["pe"] += 1
                    mm_ok = cnt["pe"]

                    @block.vector
                    def _(vector, mc=mc, bank=bank, n=n, p=p, mm_ok=mm_ok):
                        vector.wait_ge(sem_pe, mm_ok)
                        vector.tensor_tensor(sx[p][:, mc, 0:n], bank[:, 0:n],
                                             sx[p][:, mc, 0:n], ALU.add).then_inc(sem_dve, 1)
                    cnt["dve"] += 1
                    wo_evacs.append((sem_dve, cnt["dve"]))

                @block.gpsimd
                def _(gpsimd, b=b, off=off, n=n, p=p, d=cnt["dve"], sp=cnt["store"]):
                    gpsimd.wait_ge(sem_dve, d)
                    gpsimd.wait_ge(sem_store, sp)
                    gpsimd.dma_start(dypart[b, :, off:off + n].rearrange("(c p) t -> p c t", p=P),
                                     sx[p][:, :, 0:n]).then_inc(sem_store, 16)
                cnt["store"] += 16
                sx_store_mark[p] = cnt["store"]

            if not do_gf:
                break
            @block.scalar
            def _(scalar):
                load_table(scalar, SET_GELU)

            @block.vector
            def _(vector):
                vector.tensor_scalar_mul(xnsum[:], xnsum[:], 1.0 / S).then_inc(sem_dve, 1)
            cnt["dve"] += 1
            mean_mark = cnt["dve"]

            if b == 0 and "gmean" in taps_out:
                @block.gpsimd
                def _(gpsimd, mm=mean_mark):
                    gpsimd.wait_ge(sem_dve, mm)
                    gpsimd.dma_start(taps_out["gmean"], xnsum[:]).then_inc(sem_store, 16)
                cnt["store"] += 16

            gf_pe_prev = [cnt["pe"], cnt["pe"]]
            gf_stage_par = [0]

            def gf_mm_layer(dram_w, nkc, nmc, rhs_tile, rhs_off, out_bank,
                            rhs_ready_dve=None, rhs_ready_act=None,
                            store_wait=0):
                for kc in range(nkc):
                    sg = gf_stage_par[0]
                    gf_stage_par[0] ^= 1

                    @block.sync
                    def _(sync, kc=kc, sg=sg, pe_need=gf_pe_prev[sg], nmc=nmc,
                          lw=cnt["load"], sw=store_wait):
                        sync.wait_ge(sem_pe, pe_need)
                        if kc == 0 and sw:
                            sync.wait_ge(sem_store, sw)
                        sync.wait_ge(sem_load, lw)
                        sync.dma_start(stage[sg][:, 0:nmc * P],
                                       dram_w[kc * P:(kc + 1) * P, :]).then_inc(sem_load, 16)
                    cnt["load"] += 16
                    lm = cnt["load"]

                    @block.tensor
                    def _(tensor, kc=kc, sg=sg, lm=lm, nmc=nmc, nkc=nkc,
                          rhs_off=rhs_off, rrd=rhs_ready_dve, rra=rhs_ready_act):
                        tensor.wait_ge(sem_load, lm)
                        if rrd is not None and kc == 0:
                            tensor.wait_ge(sem_dve, rrd)
                        if rra is not None and kc == 0:
                            tensor.wait_ge(sem_act, rra)
                        for mc in range(nmc):
                            ins = tensor.matmul(out_bank[:, mc:mc + 1],
                                                stage[sg][:, mc * P:(mc + 1) * P],
                                                rhs_tile[:, rhs_off + kc:rhs_off + kc + 1],
                                                start=(kc == 0 and mc == 0),
                                                stop=(kc == nkc - 1),
                                                skip_group_check=True)
                        ins.then_inc(sem_pe, 1)
                    cnt["pe"] += 1
                    gf_pe_prev[sg] = cnt["pe"]
                return cnt["pe"]

            l1 = gf_mm_layer(d_g1, 8, 4, xnsum, 0, PS[0], rhs_ready_dve=mean_mark,
                             store_wait=(cnt["store"] if b > 0 else 0))

            @block.scalar
            def _(scalar, l1=l1):
                scalar.wait_ge(sem_pe, l1)
                for mc in range(4):
                    scalar.activation(gtmp[:, mc:mc + 1], PS[0][:, mc:mc + 1], AF.Gelu,
                                      bias=g1c[:, mc:mc + 1]).then_inc(sem_act, 1)
            cnt["act"] += 4

            l2 = gf_mm_layer(d_g2, 4, 2, gtmp, 0, PS[1], rhs_ready_act=cnt["act"])

            @block.scalar
            def _(scalar, l2=l2):
                scalar.wait_ge(sem_pe, l2)
                for mc in range(2):
                    scalar.activation(gtmp[:, 8 + mc:9 + mc], PS[1][:, mc:mc + 1], AF.Gelu,
                                      bias=g2c[:, mc:mc + 1]).then_inc(sem_act, 1)
            cnt["act"] += 2

            pe_need = cnt["pe"]

            @block.sync
            def _(sync, pe_need=pe_need, lw=cnt["load"]):
                sync.wait_ge(sem_pe, pe_need)
                sync.wait_ge(sem_load, lw)
                sync.dma_start(stage[0][:, 0:1], d_g3[0:P, :]).then_inc(sem_load, 16)
                sync.wait_ge(sem_load, lw + 16)
                sync.dma_start(stage[0][:, 1:2], d_g3[P:2 * P, :]).then_inc(sem_load, 16)
            cnt["load"] += 32
            lm = cnt["load"]

            @block.tensor
            def _(tensor, lm=lm, aa=cnt["act"]):
                tensor.wait_ge(sem_load, lm)
                tensor.wait_ge(sem_act, aa)
                for kc in range(2):
                    ins = tensor.matmul(PS[2][0:1, 0:1], stage[0][:, kc:kc + 1],
                                        gtmp[:, 8 + kc:9 + kc],
                                        start=(kc == 0), stop=(kc == 1),
                                        skip_group_check=True)
                ins.then_inc(sem_pe, 1)
            cnt["pe"] += 1
            u_mark = cnt["pe"]

            @block.scalar
            def _(scalar, um=u_mark):
                scalar.wait_ge(sem_pe, um)
                scalar.activation(sc[0:1, 3:4], PS[2][0:1, 0:1], AF.Tanh,
                                  bias=sc[0:1, 1:2], scale=0.5).then_inc(sem_act, 1)
            cnt["act"] += 1
            gf_mark = cnt["act"]

            @block.vector
            def _(vector, gm=gf_mark):
                vector.wait_ge(sem_act, gm)
                ins = vector.tensor_scalar(sc[0:1, 3:4], sc[0:1, 3:4], 0.5, 0.5,
                                           ALU.mult, ALU.add)
                dve_edge_on(vector, ins)
                ins = vector.tensor_tensor(sc[0:1, 7:8], sc[0:1, 2:3], sc[0:1, 3:4], ALU.mult)
                dve_edge_on(vector, ins)
                ins = vector.scalar_tensor_tensor(sc[0:1, 4:5], sc[0:1, 7:8], 1.4,
                                                  sc[0:1, 2:3], ALU.mult, ALU.subtract)
                dve_edge_on(vector, ins)
                ins = vector.tensor_scalar_add(sc[0:1, 4:5], sc[0:1, 4:5], 0.1)
                dve_edge_on(vector, ins)
                ins = vector.tensor_scalar_mul(sc[0:1, 6:7], sc[0:1, 2:3], 0.3)
                dve_edge_on(vector, ins)
                vector.tensor_tensor(sc[0:1, 5:6], sc[0:1, 4:5], sc[0:1, 6:7], ALU.add)
                vector.memset(xnsum[:], 0.0).then_inc(sem_dve, 1)
            cnt["dve"] += 1
            gf_done = cnt["dve"]

            if b == 0 and "gfsc" in taps_out:
                @block.gpsimd
                def _(gpsimd, gfd=gf_done):
                    gpsimd.wait_ge(sem_dve, gfd)
                    gpsimd.dma_start(taps_out["gfsc"], sc[:]).then_inc(sem_store, 16)
                cnt["store"] += 16

            def emit_cycle(si):
                n = SEG_N[si]
                p = si % 2

                @block.scalar
                def _(scalar, n=n, p=p, orm=out_rd[p]):
                    scalar.wait_ge(sem_dve, orm)
                    scalar.copy(sbf[p][:, :, 0:n], sx[p][:, :, 0:n]).then_inc(sem_act, 1)
                cnt["act"] += 1
                obf_mark = cnt["act"]

                def ffn_layer(wt, src, dst, bias_c, nkc, nmc, in_ready_act):
                    evacs = []
                    for mc in range(nmc):
                        bank = (PS[2], PS[3], PS[6], PS[7])[mc % 4]

                        @block.tensor
                        def _(tensor, mc=mc, bank=bank, n=n, ira=in_ready_act,
                              nkc=nkc, wt=wt, src=src):
                            if mc == 0:
                                tensor.wait_ge(sem_act, ira)
                            if mc >= 4:
                                tensor.wait_ge(sem_act, evacs[mc - 4])
                            for kc in range(nkc):
                                ins = tensor.matmul(bank[:, 0:n], wt[:, kc, mc, :],
                                                    src[:, kc, 0:n],
                                                    start=(kc == 0), stop=(kc == nkc - 1))
                            ins.then_inc(sem_pe, 1)
                        cnt["pe"] += 1
                        mm_ok = cnt["pe"]

                        @block.scalar
                        def _(scalar, mc=mc, bank=bank, n=n, mo=mm_ok, dst=dst,
                              bias_c=bias_c):
                            scalar.wait_ge(sem_pe, mo)
                            scalar.activation(dst[:, mc, 0:n], bank[:, 0:n], AF.Gelu,
                                              bias=bias_c[:, mc:mc + 1]).then_inc(sem_act, 1)
                        cnt["act"] += 1
                        evacs.append(cnt["act"])
                    return evacs

                m1 = ffn_layer(c1, sbf[p], h1, b1c, 8, 8, obf_mark)[-1]
                m2 = ffn_layer(c2, h1, h2, b2c, 8, 4, m1)[-1]
                e3 = ffn_layer(c3, h2, h3, b3c, 4, 2, m2)

                @block.tensor
                def _(tensor, n=n, e3=e3):
                    for kc in range(2):
                        tensor.wait_ge(sem_act, e3[kc])
                        ins = tensor.matmul(PS[4][0:1, 0:n], c4[:, kc:kc + 1],
                                            h3[:, kc, 0:n],
                                            start=(kc == 0), stop=(kc == 1))
                    ins.then_inc(sem_pe, 1)
                cnt["pe"] += 1
                z_mark = cnt["pe"]

                @block.scalar
                def _(scalar, n=n, zm=z_mark):
                    scalar.wait_ge(sem_pe, zm)
                    scalar.activation(t_row[0:1, 0:n], PS[4][0:1, 0:n], AF.Tanh,
                                      bias=sc[0:1, 0:1], scale=0.5).then_inc(sem_act, 1)
                cnt["act"] += 1
                tr_mark = cnt["act"]

                @block.vector
                def _(vector, n=n, tm=tr_mark):
                    vector.wait_ge(sem_act, tm)
                    vector.tensor_scalar(t_row[0:1, 0:n], t_row[0:1, 0:n],
                                         sc[0:1, 6:7], sc[0:1, 5:6],
                                         ALU.mult, ALU.add).then_inc(sem_dve, 1)
                cnt["dve"] += 1
                cp_mark = cnt["dve"]

                @block.tensor
                def _(tensor, n=n, cm=cp_mark):
                    tensor.wait_ge(sem_dve, cm)
                    tensor.matmul(PS[5][:, 0:n], ones_f[0:1, 0:P], t_row[0:1, 0:n],
                                  start=True, stop=False, skip_group_check=True)
                    tensor.matmul(PS[5][:, 0:n], zeros_f[0:1, 0:P], t_row[0:1, 0:n],
                                  start=False, stop=True,
                                  skip_group_check=True).then_inc(sem_pe, 1)
                cnt["pe"] += 1
                cpb_mark = cnt["pe"]

                @block.vector
                def _(vector, n=n, p=p, cb=cpb_mark):
                    vector.wait_ge(sem_pe, cb)
                    cpb = PS[5][:, None, 0:n].broadcast_to([P, 2, n])
                    q = lambda i: sx[p][:, 2 * i:2 * i + 2, 0:n]
                    t0 = tw0[:, 0:2 * n].rearrange("p (a t) -> p a t", a=2)
                    t1 = tw1[:, 0:2 * n].rearrange("p (a t) -> p a t", a=2)
                    vector.tensor_tensor(t0, cpb, q(0), ALU.mult)
                    ins = vector.tensor_tensor(t1, cpb, q(2), ALU.mult)
                    dve_edge_on(vector, ins)
                    vector.tensor_tensor(q(0), q(0), t1, ALU.add)
                    ins = vector.tensor_tensor(q(2), q(2), t0, ALU.subtract)
                    dve_edge_on(vector, ins)
                    vector.tensor_tensor(t0, cpb, q(1), ALU.mult)
                    ins = vector.tensor_tensor(t1, cpb, q(3), ALU.mult)
                    dve_edge_on(vector, ins)
                    vector.tensor_tensor(q(1), q(1), t1, ALU.subtract)
                    ins = vector.tensor_tensor(q(3), q(3), t0, ALU.add)
                    ins.then_inc(sem_dve, 1)
                cnt["dve"] += 1
                out_rd[p] = cnt["dve"]

            def emit_ypart(si, store_wait, pe_wait=None):
                n = SEG_N[si]
                off = SEG_OFF[si]

                @block.sync
                def _(sync, b=b, off=off, n=n, sw=store_wait, pw=pe_wait,
                      lw=cnt["load"]):
                    sync.wait_ge(sem_store, sw)
                    if pw is not None:
                        sync.wait_ge(sem_pe, pw)
                    sync.wait_ge(sem_load, lw)
                    sync.dma_start(s_big[:, :, 0:n],
                                   dypart[b, :, off:off + n].rearrange("(c p) t -> p c t", p=P)
                                   ).then_inc(sem_load, 16)
                cnt["load"] += 16
                return cnt["load"]

            def emit_final(si, ypl):
                n = SEG_N[si]
                p = si % 2

                @block.vector
                def _(vector, n=n, p=p, orm=out_rd[p], ypl=ypl):
                    vector.wait_ge(sem_dve, orm)
                    vector.wait_ge(sem_load, ypl)
                    vector.tensor_tensor(s_big[:, :, 0:n], sx[p][:, :, 0:n],
                                         s_big[:, :, 0:n], ALU.add).then_inc(sem_dve, 1)
                cnt["dve"] += 1
                return cnt["dve"]

            def emit_ystore(si, final_dve):
                n = SEG_N[si]
                off = SEG_OFF[si]

                @block.gpsimd
                def _(gpsimd, b=b, off=off, n=n, d=final_dve, sp=cnt["store"]):
                    gpsimd.wait_ge(sem_dve, d)
                    gpsimd.wait_ge(sem_store, sp)
                    gpsimd.dma_start(dy[b, :, off:off + n].rearrange("(c p) t -> p c t", p=P),
                                     s_big[:, :, 0:n]).then_inc(sem_store, 16)
                cnt["store"] += 16
                return cnt["store"]

            rows_ld[0] = emit_rows_load(b, 0, 0, cnt["store"])
            x_ld[0] = emit_xload(b, 0, 0, sx_store_mark[0])
            if n_seg_b > 1:
                rows_ld[1] = emit_rows_load(b, 1, 1, cnt["store"])
                x_ld[1] = emit_xload(b, 1, 1, sx_store_mark[1])
            bc2 = [0, 0]
            out_rd = [0, 0]
            bc2[0] = emit_bcast(0, 0, (sem_load, rows_ld[0]), gf_done, extra_act=cnt["act"])
            out_rd[0] = emit_apply(0, 0, True, bc2[0], x_ld[0], 0)
            if n_seg_b > 1:
                bc2[1] = emit_bcast(1, 1, (sem_load, rows_ld[1]), out_rd[0], extra_act=cnt["act"])
                out_rd[1] = emit_apply(1, 1, True, bc2[1], x_ld[1], 0)
            sbig_store = 0
            sbig_pe = u_mark
            final_dve = [0, 0]

            for s0 in range(0, n_seg_b, 2):
                s1 = s0 + 1 if s0 + 1 < n_seg_b else None
                yp0 = emit_ypart(s0, sbig_store, pe_wait=sbig_pe)
                sbig_pe = None
                for cyc in range(n_cyc):
                    last = (cyc == n_cyc - 1)
                    emit_cycle(s0)
                    if last:
                        final_dve[0] = emit_final(s0, yp0)
                        st0 = emit_ystore(s0, final_dve[0])
                        if s0 + 2 < n_seg_b:
                            rows_ld[0] = emit_rows_load(b, s0 + 2, 0, 0,
                                                        pe_wait=bc2[0])
                            x_ld[0] = emit_xload(b, s0 + 2, 0, 0,
                                                 dve_wait=final_dve[0])
                            bc2[0] = emit_bcast(s0 + 2, 0, (sem_load, rows_ld[0]),
                                                out_rd[1 if s1 is not None else 0],
                                                extra_act=cnt["act"])
                            out_rd[0] = emit_apply(s0 + 2, 0, True, bc2[0],
                                                   x_ld[0], 0)
                    if s1 is not None:
                        emit_cycle(s1)
                        if last:
                            yp1 = emit_ypart(s1, st0)
                            final_dve[1] = emit_final(s1, yp1)
                            sbig_store = emit_ystore(s1, final_dve[1])
                            if s1 + 2 < n_seg_b:
                                rows_ld[1] = emit_rows_load(b, s1 + 2, 1, 0,
                                                            pe_wait=bc2[1])
                                x_ld[1] = emit_xload(b, s1 + 2, 1, 0,
                                                     dve_wait=final_dve[1])
                                bc2[1] = emit_bcast(s1 + 2, 1, (sem_load, rows_ld[1]),
                                                    out_rd[0],
                                                    extra_act=cnt["act"])
                                out_rd[1] = emit_apply(s1 + 2, 1, True, bc2[1],
                                                       x_ld[1], 0)
                if s1 is None:
                    sbig_store = st0
            sx_free_dve[0] = final_dve[0]
            sx_free_dve[1] = final_dve[1] if n_seg_b > 1 else final_dve[0]

        @block.sync
        def _(sync, tot=cnt["store"]):
            sync.wait_ge(sem_store, tot)

    return nc


def _mask01():
    m = np.zeros((P, P), dtype=ml_dtypes.bfloat16)
    for c in range(8):
        m[16 * c:16 * (c + 1), 16 * c:16 * (c + 1)] = 1
    return m


def _prep_inputs(inp):
    f4 = np.float32
    x = np.asarray(inp["L0_attn_x"], dtype=f4)
    Wq = np.asarray(inp["L0_attn_Wq"], dtype=f4); Wk = np.asarray(inp["L0_attn_Wk"], dtype=f4)
    Wv = np.asarray(inp["L0_attn_Wv"], dtype=f4); Wo = np.asarray(inp["L0_attn_Wo"], dtype=f4)
    bq = np.asarray(inp["L0_attn_bq"], dtype=f4); bk = np.asarray(inp["L0_attn_bk"], dtype=f4)
    bv = np.asarray(inp["L0_attn_bv"], dtype=f4); bo = np.asarray(inp["L0_attn_bo"], dtype=f4)

    def col(v, nch):
        return np.ascontiguousarray(np.asarray(v, dtype=f4).reshape(nch, P).T)

    shared = dict(
        wq_t=np.ascontiguousarray(Wq.T), wk_t=np.ascontiguousarray(Wk.T),
        wv_t=np.ascontiguousarray(Wv.T), wo_t=np.ascontiguousarray(Wo.T),
        c1=np.ascontiguousarray(np.asarray(inp["L0_ffn_cn_w1"], dtype=f4)),
        c2=np.ascontiguousarray(np.asarray(inp["L0_ffn_cn_w2"], dtype=f4)),
        c3=np.ascontiguousarray(np.asarray(inp["L0_ffn_cn_w3"], dtype=f4)),
        c4=np.ascontiguousarray(np.asarray(inp["L0_ffn_cn_w4"], dtype=f4).reshape(256, 1)),
        g1=np.ascontiguousarray(np.asarray(inp["L0_ffn_gc_w1"], dtype=f4)),
        g2=np.ascontiguousarray(np.asarray(inp["L0_ffn_gc_w2"], dtype=f4)),
        g3=np.ascontiguousarray(np.asarray(inp["L0_ffn_gc_w3"], dtype=f4).reshape(256, 1)),
        b1c=col(inp["L0_ffn_cn_b1"], 8), b2c=col(inp["L0_ffn_cn_b2"], 4),
        b3c=col(inp["L0_ffn_cn_b3"], 2),
        g1c=col(inp["L0_ffn_gc_b1"], 4), g2c=col(inp["L0_ffn_gc_b2"], 2),
        bvr=np.ascontiguousarray(bv.reshape(1, D)),
        bor=np.ascontiguousarray(bo.reshape(1, D)),
        bqr=np.ascontiguousarray(bq.reshape(1, D)),
        bkr=np.ascontiguousarray(bk.reshape(1, D)),
        scal=np.array([[np.asarray(inp["L0_ffn_cn_b4"], dtype=f4).reshape(-1)[0],
                        np.asarray(inp["L0_ffn_gc_b3"], dtype=f4).reshape(-1)[0],
                        np.asarray(inp["L0_ffn_adaptive_range"], dtype=f4).reshape(-1)[0],
                        0.0]], dtype=f4),
        consts=np.tile(np.array([[EPS, 0.0]], dtype=f4), (P, 1)),
        mask01=_mask01(),
        identf=np.eye(P, dtype=f4),
        ones_bf=np.ones((1, 384), dtype=ml_dtypes.bfloat16),
        ones_f=np.ones((1, P), dtype=f4),
        zeros_f=np.zeros((1, P), dtype=f4),
        ones_col=np.ones((P, 1), dtype=ml_dtypes.bfloat16),
    )
    xfm = np.ascontiguousarray(x.transpose(0, 2, 1))
    return xfm, shared


def kernel(**inputs):
    xfm, shared = _prep_inputs(inputs)
    nc = build_nc()
    in_maps = []
    for i in range(NCORES):
        m = dict(shared)
        m["x"] = np.ascontiguousarray(xfm[2 * i:2 * i + 2])
        in_maps.append(m)
    res = run_bass_kernel_spmd(nc, in_maps, core_ids=list(range(NCORES)))
    y = np.stack([np.asarray(r["y"]) for r in res.results])
    y = y.reshape(16, D, S).transpose(0, 2, 1)
    return np.ascontiguousarray(y)


if __name__ == "__main__":
    import reference as _ref
    inp = _ref.setup_inputs(0)
    inp = {k: np.asarray(v) for k, v in inp.items()}
    out = kernel(**inp)
    want = np.asarray(_ref.reference(**inp))
    d64 = (out.astype(np.float64) - want.astype(np.float64))
    print("var ratio:", float((d64 ** 2).mean() / (want.astype(np.float64) ** 2).mean()))
    print("max abs:", float(np.abs(d64).max()))
